# version 09
# baseline (speedup 1.0000x reference)
import numpy as np
import ml_dtypes
import concourse.bass as bass
import concourse.mybir as mybir
from concourse.bass_utils import run_bass_kernel_spmd
from contextlib import ExitStack

F32 = mybir.dt.float32
BF16 = mybir.dt.bfloat16
FP16 = mybir.dt.float16
FP8 = mybir.dt.float8e4
AF = mybir.ActivationFunctionType
ALU = mybir.AluOpType

S = 2048
H = 1024
NH = 16
NT = 16
NF = 8
SCALE = 64.0 ** -0.5
EPS = 1e-5

ENGS = ("tensor", "vector", "scalar", "sync", "gpsimd")


class Sem:
    def __init__(self, handle):
        self.h = handle
        self.count = 0


class Prog:

    def __init__(self, nc, ctx):
        self.nc = nc
        self.ctx = ctx
        self.streams = {e: [] for e in ENGS}
        self.sems = {}
        self._last_wait = {}

    def sem(self, name):
        if name not in self.sems:
            self.sems[name] = Sem(self.ctx.enter_context(self.nc.semaphore(name)))
        return self.sems[name]

    def emit(self, eng, fn, waits=(), incs=()):
        wl = []
        for s, v in waits:
            key = (eng, s.h.name)
            if v > 0 and self._last_wait.get(key, -1) < v:
                self._last_wait[key] = v
                wl.append((s, v))
        for s, k in incs:
            s.count += k
        self.streams[eng].append((fn, wl, list(incs)))

    def finalize(self):
        block = self.ctx.enter_context(self.nc.Block())
        streams = self.streams

        def runner(name):
            def _run(eng):
                with self.nc.allow_low_precision(
                        reason="intentional bf16 stores; accumulation stays fp32"):
                    for fn, waits, incs in streams[name]:
                        for s, v in waits:
                            eng.wait_ge(s.h, v)
                        inst = fn(eng)
                        for s, k in incs:
                            inst.then_inc(s.h, k)
            return _run

        block.tensor(runner("tensor"))
        block.vector(runner("vector"))
        block.scalar(runner("scalar"))
        block.sync(runner("sync"))
        block.gpsimd(runner("gpsimd"))


def _tri_masks():
    tl = np.arange(128)[:, None]
    sl = np.arange(128)[None, :]
    tri_hi = (tl <= sl).astype(np.float32)
    tri_lo = (tl >= sl).astype(np.float32)
    tri_lo_r0 = np.maximum(tri_lo, np.broadcast_to((tl == 0), (128, 128)).astype(np.float32))
    tri_hi_c0 = np.maximum(tri_hi, np.broadcast_to((sl == 0), (128, 128)).astype(np.float32))
    m = np.zeros((128, 1024), np.float32)
    m[:, 0:128] = tri_hi
    m[:, 128:256] = tri_lo
    m[:, 256:384] = tri_hi_c0
    m[:, 384:512] = tri_lo
    m[:, 512:640] = tri_lo_r0
    m[:, 640:768] = tri_hi
    return m.astype(ml_dtypes.bfloat16)


def build(debug=False, phase_limit=None):
    nc = bass.Bass("TRN2", target_bir_lowering=False)

    x_d = nc.dram_tensor("x", [S, H], F32, kind="ExternalInput").ap()
    wqT_d = nc.dram_tensor("wqT", [H, H], F32, kind="ExternalInput").ap()
    wkT_d = nc.dram_tensor("wkT", [H, H], F32, kind="ExternalInput").ap()
    wvT_d = nc.dram_tensor("wvT", [H, H], F32, kind="ExternalInput").ap()
    woT_d = nc.dram_tensor("woT", [H, H], F32, kind="ExternalInput").ap()
    wfT_d = nc.dram_tensor("wfT", [H, H], F32, kind="ExternalInput").ap()
    biasr_d = nc.dram_tensor("biasr", [128, H], F32, kind="ExternalInput").ap()
    masks_d = nc.dram_tensor("masks", [128, 1024], BF16, kind="ExternalInput").ap()
    idbf_d = nc.dram_tensor("idbf", [128, 128], BF16, kind="ExternalInput").ap()
    id8_d = nc.dram_tensor("id8", [128, 128], FP8, kind="ExternalInput").ap()
    idf16_d = nc.dram_tensor("idf16", [128, 128], FP16, kind="ExternalInput").ap()
    onesbf_d = nc.dram_tensor("onesbf", [128, 128], BF16, kind="ExternalInput").ap()
    y_d = nc.dram_tensor("y", [S, H], F32, kind="ExternalOutput").ap()

    dbg = {}
    if debug:
        for name, shape, dt in [
            ("d_xnT", [128, NF * S], BF16), ("d_xqT", [128, NF * S], FP8),
            ("d_qT", [128, NF * S], BF16), ("d_kT", [128, NF * S], BF16),
            ("d_v", [128, NT * 1040], BF16), ("d_attnT", [128, NF * S], BF16),
            ("d_ffn", [128, NF * S], FP16), ("d_e0", [97, 4 * S], BF16),
            ("d_e0q", [128, 256], BF16), ("d_stats", [128, 6 * NT], F32),
        ]:
            dbg[name] = nc.dram_tensor(name, shape, dt, kind="ExternalOutput").ap()

    with ExitStack() as ctx:
        e = ctx.enter_context
        xnT = e(nc.sbuf_tensor("xnT_attnT", [128, NF * S], BF16))
        attnT = xnT
        qtA = e(nc.sbuf_tensor("qt_ffn", [128, NF * S], BF16))
        QT = qtA
        ffn16 = qtA.bitcast(FP16)
        ktA = e(nc.sbuf_tensor("kt_xre", [128, NF * S], BF16))
        KT = ktA
        x_re = ktA.bitcast(F32)
        vA = e(nc.sbuf_tensor("v_out", [128, NT * 1040], BF16))
        V = vA
        out_f32 = vA.bitcast(F32)
        wbfA = e(nc.sbuf_tensor("wbf_a", [128, NF * H], BF16))
        E0 = wbfA
        wbfB = e(nc.sbuf_tensor("wbf_b", [128, NF * H], BF16))
        wf8 = e(nc.sbuf_tensor("wf8", [128, NF * H], FP8))
        xqT = e(nc.sbuf_tensor("xqT", [128, NF * S], FP8))
        otbf = xqT.bitcast(BF16)

        masks = e(nc.sbuf_tensor("sb_masks", [128, 768], BF16))
        idbf = e(nc.sbuf_tensor("sb_idbf", [128, 128], BF16))
        id8 = e(nc.sbuf_tensor("sb_id8", [128, 128], FP8))
        idf16 = e(nc.sbuf_tensor("sb_idf16", [128, 128], FP16))
        onesbf = e(nc.sbuf_tensor("sb_onesbf", [128, 128], BF16))
        Gc = e(nc.sbuf_tensor("Gc", [128, 8 * 97], BF16))
        Gq = e(nc.sbuf_tensor("Gq", [128, 8 * 16], BF16))
        Vg = e(nc.sbuf_tensor("Vg", [97, 1040], BF16))
        E0qT = e(nc.sbuf_tensor("E0qT", [128, 256], BF16))
        Esb = e(nc.sbuf_tensor("Esb", [128, 6 * 448], BF16))
        scA = e(nc.sbuf_tensor("scA", [128, S], BF16))
        Drow = e(nc.sbuf_tensor("Drow", [65, S], BF16))
        Dsp = e(nc.sbuf_tensor("Dsp", [128, 16], BF16))
        Dspr = e(nc.sbuf_tensor("Dspr", [128, 16], BF16))
        rc = e(nc.sbuf_tensor("rc", [128, 2 * 512], BF16))
        sumc = e(nc.sbuf_tensor("sumc", [128, NT], F32))
        cinvH = e(nc.sbuf_tensor("cinvH", [128, 4], F32))
        cnegH = e(nc.sbuf_tensor("cnegH", [128, 4], F32))
        cepsm = e(nc.sbuf_tensor("cepsm", [128, 4], F32))
        sqc = e(nc.sbuf_tensor("sqc", [128, NT], F32))
        zb = e(nc.sbuf_tensor("zb", [128, 512], BF16))
        mcol = e(nc.sbuf_tensor("mcol", [128, NT], F32))
        vcol = e(nc.sbuf_tensor("vcol", [128, NT], F32))
        tcol = sumc
        rstd = vcol
        nmr = tcol

        pb = [e(nc.psum_tensor(f"pb{i}", [128, 512], F32)) for i in range(4)]
        ps = [e(nc.psum_tensor(f"ps{i}", [128, 512], F32)) for i in range(3)]
        pr = e(nc.psum_tensor("pr", [128, 512], F32))
        psum_rot = pb + ps + [pr]

        P = Prog(nc, ctx)
        s_ldc = P.sem("ldc")
        s_ldx = P.sem("ldx")
        s_ldw = P.sem("ldw")
        s_pe = P.sem("pe")
        s_ve = P.sem("ve")
        s_ac = P.sem("ac")
        s_gp = P.sem("gp")
        s_st = P.sem("st")
        s_vg = P.sem("vg")
        s_xr = P.sem("xr")
        s_dsc = P.sem("dsc")

        biasr = ktA.bitcast(F32)[:, 4096:5120]
        for dst, src in [(masks, masks_d[:, 0:768]), (idbf, idbf_d), (id8, id8_d),
                         (idf16, idf16_d), (onesbf, onesbf_d)]:
            P.emit("sync", lambda en, d=dst, s=src: en.dma_start(out=d[:], in_=s),
                   incs=[(s_ldc, 16)])
        C_CONST = s_ldc.count

        def wload(dst, src):
            return lambda en: en.dma_start(
                out=dst[:].rearrange("p (ic o) -> p ic o", o=H),
                in_=src.rearrange("(ic p) o -> p ic o", p=128))

        P.emit("gpsimd", wload(wbfA, wqT_d), incs=[(s_ldw, 16)])
        CW_Q = s_ldw.count
        P.emit("gpsimd", wload(wbfB, wkT_d), incs=[(s_ldw, 16)])
        CW_K = s_ldw.count
        P.emit("gpsimd", wload(wf8, wfT_d), incs=[(s_ldw, 16)])
        CW_F = s_ldw.count

        x_sb = qtA.bitcast(F32)
        xn_tok = ktA
        xq_tok = ktA.bitcast(FP8)
        xd3 = x_d.rearrange("(c t p) h -> c p t h", p=128, t=4)

        rot_i = 0
        rot_free = [0] * 8
        chunk_done = {}
        P.emit("vector", lambda en: en.memset(cinvH[:], 1.0 / H))
        P.emit("vector", lambda en: en.memset(cnegH[:], -1.0 / H))
        P.emit("vector", lambda en: en.memset(cepsm[:], float(EPS)), incs=[(s_ve, 1)])
        C_MEMS = s_ve.count

        kt_f32 = ktA.bitcast(F32)

        def xslot(c):
            if c == 2:
                return kt_f32[:, 4096:8192]
            sl = 0 if c == 3 else c
            return x_sb[:, sl * 4096:(sl + 1) * 4096]

        xdma_count = {}
        for c in range(3):
            P.emit("sync", lambda en, c=c: en.dma_start(
                out=xslot(c).rearrange("p (t h) -> p t h", h=1024), in_=xd3[c]),
                incs=[(s_ldx, 16)])
            xdma_count[c] = s_ldx.count

        for c in range(4):
            if c == 1:
                P.emit("sync", lambda en: en.dma_start(
                    out=xslot(3).rearrange("p (t h) -> p t h", h=1024), in_=xd3[3]),
                    waits=[(s_ve, chunk_done[0])], incs=[(s_ldx, 16)])
                xdma_count[3] = s_ldx.count
            C_X = xdma_count[c]
            xsq = ktA.bitcast(F32)
            for i in range(4):
                t = 4 * c + i
                xt = xslot(c)[:, i * 1024:(i + 1) * 1024]
                P.emit("vector", lambda en, xt=xt, t=t: en.reduce_sum(
                    sumc[:, t:t + 1], xt, axis=mybir.AxisListType.X),
                    waits=[(s_ldx, C_X)])
                P.emit("scalar", lambda en, xt=xt: en.activation(
                    xsq[:, 3072:4096], xt, AF.Square),
                    waits=[(s_ldx, C_X)] if i == 0 else [(s_ve, s_ve.count)],
                    incs=[(s_ac, 1)])
                C_SQ = s_ac.count
                P.emit("vector", lambda en, t=t: en.reduce_sum(
                    sqc[:, t:t + 1], xsq[:, 3072:4096], axis=mybir.AxisListType.X),
                    waits=[(s_ac, C_SQ)], incs=[(s_ve, 1)])
            C_RED = s_ve.count
            t0, t1 = 4 * c, 4 * c + 4
            sl = slice(t0, t1)
            P.emit("gpsimd", lambda en, sl=sl: en.tensor_mul(mcol[:, sl], sumc[:, sl], cnegH[:, 0:4]),
                   waits=[(s_ve, C_RED), (s_ve, C_MEMS)])
            P.emit("gpsimd", lambda en, sl=sl: en.tensor_mul(tcol[:, sl], mcol[:, sl], mcol[:, sl]))
            P.emit("gpsimd", lambda en, sl=sl: en.tensor_mul(vcol[:, sl], sqc[:, sl], cinvH[:, 0:4]))
            P.emit("gpsimd", lambda en, sl=sl: en.tensor_add(vcol[:, sl], vcol[:, sl], cepsm[:, 0:4]))
            P.emit("gpsimd", lambda en, sl=sl: en.tensor_sub(vcol[:, sl], vcol[:, sl], tcol[:, sl]),
                   incs=[(s_gp, 1)])
            C_V = s_gp.count
            P.emit("vector", lambda en, sl=sl: en.reciprocal(tcol[:, sl], vcol[:, sl]),
                   waits=[(s_gp, C_V)], incs=[(s_ve, 1)])
            C_RV = s_ve.count
            P.emit("scalar", lambda en, sl=sl: en.activation(rstd[:, sl], tcol[:, sl], AF.Sqrt),
                   waits=[(s_ve, C_RV)], incs=[(s_ac, 1)])
            C_RS = s_ac.count
            P.emit("gpsimd", lambda en, sl=sl: en.tensor_mul(nmr[:, sl], mcol[:, sl], rstd[:, sl]),
                   waits=[(s_ac, C_RS)], incs=[(s_gp, 1)])
            C_NMR = s_gp.count

            for i in range(4):
                t = 4 * c + i
                xt = xslot(c)[:, i * 1024:(i + 1) * 1024]
                P.emit("scalar", lambda en, xt=xt, i=i, t=t: en.activation(
                    xn_tok[:, i * 1024:(i + 1) * 1024], xt, AF.Identity,
                    scale=rstd[:, t:t + 1], bias=nmr[:, t:t + 1]),
                    waits=[(s_gp, C_NMR)])
                P.emit("scalar", lambda en, xt=xt, i=i, t=t: en.activation(
                    xq_tok[:, 8192 + i * 1024: 8192 + (i + 1) * 1024], xt, AF.Identity,
                    scale=rstd[:, t:t + 1], bias=nmr[:, t:t + 1]),
                    incs=[(s_ac, 1)])
            C_LN = s_ac.count

            for kind in range(2):
                for f in range(NF):
                    bi = rot_i % 8
                    bank = psum_rot[bi]
                    for i in range(4):
                        wv = []
                        if i == 0:
                            wv = [(s_ac, C_LN), (s_ldc, C_CONST)]
                            if rot_free[bi] > 0:
                                wv.append((s_ve, rot_free[bi]))
                        if kind == 0:
                            lhsT = xn_tok[:, i * 1024 + f * 128:i * 1024 + (f + 1) * 128]
                            rhs = idbf[:]
                        else:
                            lhsT = xq_tok[:, 8192 + i * 1024 + f * 128: 8192 + i * 1024 + (f + 1) * 128]
                            rhs = id8[:]
                        P.emit("tensor", lambda en, b=bank, i=i, l=lhsT, r=rhs: en.matmul(
                            b[:, i * 128:(i + 1) * 128], l, r, start=True, stop=True),
                            waits=wv, incs=[(s_pe, 1)] if i == 3 else ())
                    C_T = s_pe.count
                    dst = (xnT if kind == 0 else xqT)[:, f * S + c * 512: f * S + (c + 1) * 512]
                    P.emit("vector", lambda en, b=bank, d=dst: en.tensor_copy(d, b[:]),
                           waits=[(s_pe, C_T)], incs=[(s_ve, 1)])
                    rot_free[bi] = s_ve.count
                    rot_i += 1
            chunk_done[c] = s_ve.count
        C_P0 = s_ve.count
        C_DBG_P0 = 0
        if debug:
            def dump_stats(en):
                return en.dma_start(out=dbg["d_stats"].rearrange("p (k t) -> p k t", t=NT),
                                    in_=bass.AP(sumc, 0, [[NT, 128], [0, 1], [1, NT]]))
            for ii, tens in enumerate([sumc, sqc, mcol, vcol, rstd, nmr]):
                P.emit("gpsimd", lambda en, ii=ii, tn=tens: en.dma_start(
                    out=dbg["d_stats"][:, ii * NT:(ii + 1) * NT], in_=tn[:]),
                    waits=[(s_ve, C_P0)], incs=[(s_vg, 16)])
            P.emit("gpsimd", lambda en: en.dma_start(out=dbg["d_xnT"], in_=xnT[:]),
                   waits=[(s_ve, C_P0)], incs=[(s_vg, 16)])
            P.emit("gpsimd", lambda en: en.dma_start(out=dbg["d_xqT"], in_=xqT[:]),
                   incs=[(s_vg, 16)])
            C_DBG_P0 = s_vg.count

        if phase_limit == "p0":
            P.emit("sync", lambda en: en.nop(), waits=[(s_vg, s_vg.count)])
            P.finalize()
            return nc

        bank_free = {0: 0, 1: 0}

        def gemm_T(wsb, out_sb, wait_w):
            for o in range(NF):
                banks = [psum_rot[(o % 2) * 4 + j] for j in range(4)]
                for ic in range(NF):
                    lhsT = wsb[:, ic * H + o * 128: ic * H + (o + 1) * 128]
                    for sc in range(4):
                        wv = []
                        if ic == 0 and sc == 0:
                            wv = [(s_ldw, wait_w), (s_ve, C_P0)]
                            if bank_free[o % 2] > 0:
                                wv.append((s_ve, bank_free[o % 2]))
                        P.emit("tensor", lambda en, b=banks[sc], l=lhsT, sc=sc, ic=ic: en.matmul(
                            b[:], l, xnT[:, ic * S + sc * 512: ic * S + (sc + 1) * 512],
                            start=(ic == 0), stop=(ic == NF - 1)),
                            waits=wv,
                            incs=[(s_pe, 1)] if (ic == NF - 1 and sc == 3) else ())
                C_G = s_pe.count
                for sc in range(4):
                    dst = out_sb[:, o * S + sc * 512: o * S + (sc + 1) * 512]
                    P.emit("vector", lambda en, b=banks[sc], d=dst: en.tensor_copy(d, b[:]),
                           waits=[(s_pe, C_G)] if sc == 0 else (), incs=[(s_ve, 1)])
                bank_free[o % 2] = s_ve.count
            return s_ve.count

        C_QT = gemm_T(wbfA, QT, CW_Q)
        P.emit("gpsimd", wload(wbfA, wvT_d),
               waits=[(s_pe, s_pe.count)], incs=[(s_ldw, 16)])
        CW_V = s_ldw.count
        C_KT = gemm_T(wbfB, KT, CW_K)
        P.emit("gpsimd", wload(wbfB, woT_d),
               waits=[(s_pe, s_pe.count)], incs=[(s_ldw, 16)])
        CW_O = s_ldw.count

        P.emit("vector", lambda en: en.memset(V[:], 1.0))
        P.emit("vector", lambda en: en.memset(zb[:], 0.0), incs=[(s_ve, 1)])
        C_VONE = s_ve.count
        vbank_free = {0: 0, 1: 0}
        C_V0 = 0
        for t in range(NT):
            banks = [psum_rot[(t % 2) * 2 + j] for j in range(2)]
            for ic in range(NF):
                lhsT = xnT[:, ic * S + t * 128: ic * S + (t + 1) * 128]
                for oc in range(2):
                    wv = []
                    if ic == 0 and oc == 0:
                        wv = [(s_ldw, CW_V), (s_ve, C_VONE), (s_ve, C_KT)]
                        if vbank_free[t % 2] > 0:
                            wv.append((s_ve, vbank_free[t % 2]))
                    P.emit("tensor", lambda en, b=banks[oc], l=lhsT, ic=ic, oc=oc: en.matmul(
                        b[:], l, wbfA[:, ic * H + oc * 512: ic * H + (oc + 1) * 512],
                        start=(ic == 0), stop=(ic == NF - 1)),
                        waits=wv,
                        incs=[(s_pe, 1)] if (ic == NF - 1 and oc == 1) else ())
            C_VG = s_pe.count
            for oc in range(2):
                dstap = V[:, t * 1040 + oc * 520: t * 1040 + (oc + 1) * 520].rearrange(
                    "p (h d) -> p h d", d=65)[:, :, 0:64]
                srcap = banks[oc][:].rearrange("p (h d) -> p h d", d=64)
                zap = zb[:].rearrange("p (h d) -> p h d", d=64)
                P.emit("vector", lambda en, sp=srcap, dp=dstap, zp=zap: en.tensor_tensor(dp, sp, zp, ALU.add),
                       waits=[(s_pe, C_VG)] if oc == 0 else (), incs=[(s_ve, 1)])
            vbank_free[t % 2] = s_ve.count
            if t == 0:
                C_V0 = s_ve.count
        C_VALL = s_ve.count
        C_DBG_P1 = 0
        if debug:
            P.emit("gpsimd", lambda en: en.dma_start(out=dbg["d_qT"], in_=QT[:]),
                   waits=[(s_ve, C_VALL)], incs=[(s_vg, 16)])
            P.emit("gpsimd", lambda en: en.dma_start(out=dbg["d_kT"], in_=KT[:]),
                   incs=[(s_vg, 16)])
            P.emit("gpsimd", lambda en: en.dma_start(out=dbg["d_v"], in_=V[:]),
                   incs=[(s_vg, 16)])
            C_DBG_P1 = s_vg.count

        for j in range(4):
            P.emit("gpsimd", lambda en, j=j: en.dma_start(out=Vg[32 * j:32 * j + 1, :], in_=V[0:1, 0:1040]),
                   waits=[(s_ve, C_V0)] if j == 0 else (), incs=[(s_vg, 16)])
        C_VGC = s_vg.count

        P.emit("vector", lambda en: en.memset(Gc[:], 0.0), waits=[(s_ve, C_KT)])
        P.emit("vector", lambda en: en.memset(Gq[:], 0.0))
        for cch in range(8):
            colA = 32 * ((2 * cch) % 4)
            colB = 32 * ((2 * cch + 1) % 4)
            P.emit("vector", lambda en, c=cch, col=colA: en.tensor_copy(
                Gc[0:64, c * 97 + col: c * 97 + col + 1], KT[0:64, c * S: c * S + 1]))
            P.emit("vector", lambda en, c=cch, col=colB: en.tensor_copy(
                Gc[64:128, c * 97 + col: c * 97 + col + 1], KT[64:128, c * S: c * S + 1]))
            P.emit("vector", lambda en, c=cch: en.tensor_copy(
                Gq[0:64, c * 16 + 2 * c: c * 16 + 2 * c + 1], QT[0:64, c * S: c * S + 1]),
                waits=[(s_ve, C_QT)] if cch == 0 else ())
            P.emit("vector", lambda en, c=cch: en.tensor_copy(
                Gq[64:128, c * 16 + 2 * c + 1: c * 16 + 2 * c + 2], QT[64:128, c * S: c * S + 1]),
                incs=[(s_ve, 1)] if cch == 7 else ())
        C_GBUILD = s_ve.count

        for tt in range(NT):
            for cch in range(NF):
                wv = []
                if tt == 0 and cch == 0:
                    wv = [(s_ve, C_GBUILD), (s_ve, C_VALL)]
                P.emit("tensor", lambda en, tt=tt, c=cch: en.matmul(
                    ps[0][:, tt * 16:(tt + 1) * 16],
                    KT[:, c * S + tt * 128: c * S + (tt + 1) * 128],
                    Gq[:, c * 16:(c + 1) * 16],
                    start=(c == 0), stop=(c == NF - 1)),
                    waits=wv,
                    incs=[(s_pe, 1)] if (tt == NT - 1 and cch == NF - 1) else ())
        C_S0Q = s_pe.count
        P.emit("scalar", lambda en: en.activation(E0qT[:], ps[0][:, 0:256], AF.Exp, scale=float(SCALE)),
               waits=[(s_pe, C_S0Q)], incs=[(s_ac, 1)])
        C_E0Q = s_ac.count

        e0_wait = {0: 0, 1: 0}
        for wv_i in range(4):
            for sc in range(4):
                k = wv_i * 4 + sc
                bank = pr if k % 2 == 0 else ps[1]
                wvl = [(s_ve, C_GBUILD)]
                if e0_wait[k % 2] > 0:
                    wvl.append((s_ac, e0_wait[k % 2]))
                for ci, cch in enumerate((2 * wv_i, 2 * wv_i + 1)):
                    P.emit("tensor", lambda en, c=cch, sc=sc, b=bank, ci=ci: en.matmul(
                        b[0:97, :], Gc[:, c * 97:(c + 1) * 97],
                        QT[:, c * S + sc * 512: c * S + (sc + 1) * 512],
                        start=(ci == 0), stop=(ci == 1)),
                        waits=wvl if ci == 0 else (), incs=[(s_pe, 1)] if ci == 1 else ())
                C_E0M = s_pe.count
                P.emit("scalar", lambda en, w=wv_i, sc=sc, b=bank: en.activation(
                    E0[0:97, w * S + sc * 512: w * S + (sc + 1) * 512], b[0:97, :],
                    AF.Exp, scale=float(SCALE)),
                    waits=[(s_pe, C_E0M)], incs=[(s_ac, 1)])
                e0_wait[k % 2] = s_ac.count
        C_E0 = s_ac.count
        if debug:
            P.emit("gpsimd", lambda en: en.dma_start(out=dbg["d_e0"], in_=E0[0:97, 0:4 * S]),
                   waits=[(s_ac, C_E0)], incs=[(s_vg, 16)])
            P.emit("gpsimd", lambda en: en.dma_start(out=dbg["d_e0q"], in_=E0qT[:]),
                   waits=[(s_ac, C_E0Q)], incs=[(s_vg, 16)])

        ps_free = [C_E0Q, e0_wait[1]]
        ebuf_free = [0, 0, 0]
        pb_free = [0, 0, 0, 0]
        pr_free = [0]
        pending_norm = []
        last_rep_pe = {-1: 0}
        chain_done = {}
        recip_done = {-1: 0}
        chain_out2 = {-1: 0}

        def emit_norm_job(h2, b2):
            hr2 = (h2 % 2) * 64
            hf2 = h2 // 2
            wv = [(s_dsc, chain_done[h2]), (s_ldc, C_CONST), (s_ac, C_E0)]
            if pr_free[0] != 0:
                wv.append((s_ac, pr_free[0][1]))
            P.emit("tensor", lambda en, b=b2, po=hr2: en.matmul(
                pr[po:po + 64, :], onesbf[64:65, 0:64], Drow[64:65, b * 512:(b + 1) * 512],
                start=True, stop=True, tile_position=(64, po)),
                waits=wv, incs=[(s_pe, 1)])
            C_REP = s_pe.count
            last_rep_pe[h2] = C_REP
            P.emit("scalar", lambda en, b=b2, po=hr2: en.copy(
                rc[po:po + 64, (b % 2) * 512:(b % 2 + 1) * 512], pr[po:po + 64, :]),
                waits=[(s_pe, C_REP)], incs=[(s_ac, 1)])
            pr_free[0] = ("ac", s_ac.count)
            dst = attnT[hr2:hr2 + 64, hf2 * S + b2 * 512: hf2 * S + (b2 + 1) * 512]
            wv2 = [(s_ac, s_ac.count)]
            if debug and C_DBG_P0 > 0:
                wv2.append((s_vg, C_DBG_P0))
            P.emit("vector", lambda en, d=dst, b=b2, po=hr2: en.tensor_tensor(
                d, scA[po:po + 64, b * 512:(b + 1) * 512],
                rc[po:po + 64, (b % 2) * 512:(b % 2 + 1) * 512], ALU.mult),
                waits=wv2, incs=[(s_ve, 1)])

        NSTEP = NH * NT
        scores_head_done = {}
        ffn_sched = [(f, sc, cc) for f in range(6) for sc in range(4) for cc in range(NF)]
        ffn_pos = [0]
        ffn_bank_free = [0]
        DEPTH = 5
        sinfo = {}
        touched_h = {}
        ebuf_free = [0, 0, 0, 0, 0, 0]

        def emit_front(st):
            h, tc = divmod(st, NT)
            hr = (h % 2) * 64
            hf = h // 2
            g = st % 2
            slot = st % 6
            if tc == 0:
                q0, q1, ocol, ncol = 0, 2, 128, 256
            elif tc == NT - 1:
                q0, q1, ocol, ncol = tc - 1, tc + 1, 0, 256
            else:
                q0, q1, ocol, ncol = tc - 1, tc + 2, 0, 384
            wv = [(s_ve, C_VALL), (s_vg, C_VGC), (s_ac, C_E0)]
            if ps_free[g] > 0:
                wv.append((s_ac, ps_free[g]))
            lhsT = KT[hr:hr + 64, hf * S + tc * 128: hf * S + (tc + 1) * 128]
            rhs = QT[hr:hr + 64, hf * S + q0 * 128: hf * S + q1 * 128]
            P.emit("tensor", lambda en, b=ps[g], l=lhsT, r=rhs, oc=ocol, nn=ncol, hr=hr: en.matmul(
                b[:, oc:oc + nn], l, r, start=True, stop=True, tile_position=(hr, 0)),
                waits=wv, incs=[(s_pe, 1)])
            C_SC = s_pe.count
            if tc == NT - 1:
                scores_head_done[h] = C_SC

            eoff, ewid = (128, 256) if tc == 0 else ((0, 256) if tc == NT - 1 else (0, 384))
            wv = [(s_pe, C_SC)]
            if ebuf_free[slot] > 0:
                wv.append((s_pe, ebuf_free[slot]))
            P.emit("scalar", lambda en, b=ps[g], slot=slot, eo=eoff, ew=ewid: en.activation(
                Esb[:, slot * 448 + eo: slot * 448 + eo + ew], b[:, eo:eo + ew],
                AF.Exp, scale=float(SCALE)),
                waits=wv, incs=[(s_ac, 1)])
            ps_free[g] = s_ac.count
            C_EXP = s_ac.count

            eng = "vector" if st % 2 == 0 else "gpsimd"
            sem = s_ve if eng == "vector" else s_gp
            if tc == 0:
                eap = Esb[:, slot * 448 + 256: slot * 448 + 384]
                map_ = masks[:, 512:640]
            elif tc == 1:
                eap = Esb[:, slot * 448: slot * 448 + 384].rearrange("p (a b) -> p a b", b=128)[:, 0:3:2, :]
                map_ = masks[:, 256:512].rearrange("p (a b) -> p a b", b=128)
            elif tc == NT - 1:
                eap = Esb[:, slot * 448: slot * 448 + 128]
                map_ = masks[:, 640:768]
            else:
                eap = Esb[:, slot * 448: slot * 448 + 384].rearrange("p (a b) -> p a b", b=128)[:, 0:3:2, :]
                map_ = masks[:, 0:256].rearrange("p (a b) -> p a b", b=128)
            P.emit(eng, lambda en, eap=eap, map_=map_: en.tensor_tensor(eap, eap, map_, ALU.mult),
                   waits=[(s_ac, C_EXP), (s_ldc, C_CONST)], incs=[(sem, 1)])
            sinfo[st] = (h, tc, slot, sem, sem.count)

        def emit_back(st):
            h, tc, slot, mask_sem, mask_cnt = sinfo.pop(st)
            hr = (h % 2) * 64
            hf = h // 2
            er0 = 32 * (h % 4)
            wv0 = h // 4
            if tc == 0:
                touched_h[h] = [False, False, False, False]
            touched = touched_h[h]
            if tc == 0:
                blocks = [(128, 0), (256, 1)]
            elif tc == NT - 1:
                blocks = [(0, tc - 1), (128, tc)]
            else:
                blocks = [(0, tc - 1), (128, tc), (256, tc + 1)]
            runs = []
            for ecol, Tq in blocks:
                b = Tq // 4
                col = (Tq % 4) * 128
                fresh = (Tq == tc + 1)
                if (runs and not fresh and runs[-1][1] == b
                        and runs[-1][2] + runs[-1][3] == col and not runs[-1][4]):
                    runs[-1] = (runs[-1][0], b, runs[-1][2], runs[-1][3] + 128, False)
                elif (runs and tc == 0 and runs[-1][1] == b
                      and runs[-1][2] + runs[-1][3] == col):
                    runs[-1] = (runs[-1][0], b, runs[-1][2], runs[-1][3] + 128, True)
                else:
                    runs.append((ecol, b, col, 128, fresh))
            lhsT_v = V[:, tc * 1040 + h * 65: tc * 1040 + (h + 1) * 65]
            first = True
            for ecol, b, col, width, _fresh in runs:
                wv = []
                if first:
                    wv.append((mask_sem, mask_cnt))
                    first = False
                st_flag = not touched[b]
                if st_flag and pb_free[b] > 0:
                    wv.append((s_ve, pb_free[b]))
                touched[b] = True
                P.emit("tensor", lambda en, bb=pb[b], l=lhsT_v, slot=slot, ec=ecol, col=col, wd=width, sf=st_flag: en.matmul(
                    bb[0:65, col:col + wd], l,
                    Esb[:, slot * 448 + ec: slot * 448 + ec + wd],
                    start=sf, stop=False),
                    waits=wv, incs=[(s_pe, 1)])
            if tc >= 2:
                P.emit("tensor", lambda en, l=lhsT_v, tc=tc, h=h: en.matmul(
                    pb[0][0:65, 0:1], l, E0qT[:, tc * 16 + h: tc * 16 + h + 1],
                    start=False, stop=(tc == 15)),
                    waits=[(s_ac, C_E0Q)], incs=[(s_pe, 1)])
            e0b = []
            if tc == 4:
                e0b = [(0, 256, 256, False)]
            elif tc == 8:
                e0b = [(1, 0, 512, True)]
            elif tc == 12:
                e0b = [(2, 0, 512, True)]
            elif tc == 15:
                e0b = [(3, 0, 512, True)]
            for b, col, width, stp in e0b:
                P.emit("tensor", lambda en, b=b, col=col, wd=width, stp=stp, er=er0, w=wv0, h=h: en.matmul(
                    pb[b][0:65, col:col + wd],
                    Vg[er:er + 1, h * 65:(h + 1) * 65],
                    E0[er:er + 1, w * S + b * 512 + col: w * S + b * 512 + col + wd],
                    start=False, stop=stp, tile_position=(er, 0)),
                    incs=[(s_pe, 1)])
            ebuf_free[slot] = s_pe.count

            if tc in (7, 9, 11, 13) and pending_norm:
                h2, b2 = pending_norm.pop(0)
                emit_norm_job(h2, b2)

            if st >= 48 and ffn_pos[0] < len(ffn_sched):
                f, sc, cc = ffn_sched[ffn_pos[0]]
                ffn_pos[0] += 1
                wvf = []
                if cc == 0:
                    wvf = [(s_ldw, CW_F)]
                    if ffn_bank_free[0] > 0:
                        wvf.append((s_ac, ffn_bank_free[0]))
                P.emit("tensor", lambda en, f=f, sc=sc, cc=cc: en.matmul(
                    ps[2][:], wf8[:, cc * H + f * 128: cc * H + (f + 1) * 128],
                    xqT[:, cc * S + sc * 512: cc * S + (sc + 1) * 512],
                    start=(cc == 0), stop=(cc == NF - 1)),
                    waits=wvf, incs=[(s_pe, 1)])
                if cc == NF - 1:
                    C_FG2 = s_pe.count
                    P.emit("scalar", lambda en, f=f, sc=sc: en.copy(
                        ffn16[:, f * S + sc * 512: f * S + (sc + 1) * 512], ps[2][:]),
                        waits=[(s_pe, C_FG2), (s_pe, scores_head_done[2 * f + 1])],
                        incs=[(s_ac, 1)])
                    ffn_bank_free[0] = s_ac.count

            jobs = []
            if tc == 8:
                jobs = [1]
            elif tc == 12:
                jobs = [2]
            elif tc == 15:
                jobs = [0, 3]
            for b in jobs:
                C_AVB = s_pe.count
                wv_e = [(s_pe, C_AVB)]
                if h > 0 and tc == 8:
                    wv_e.append((s_dsc, chain_done[h - 1]))
                P.emit("vector", lambda en, b=b, hr=hr: en.tensor_copy(
                    scA[hr:hr + 64, b * 512:(b + 1) * 512], pb[b][0:64, :]),
                    waits=wv_e)
                P.emit("vector", lambda en, b=b: en.tensor_copy(
                    Drow[64:65, b * 512:(b + 1) * 512], pb[b][64:65, :]),
                    incs=[(s_ve, 1)])
                pb_free[b] = s_ve.count
            if tc == 15:
                C_EV = s_ve.count
                drow_ap = bass.AP(Drow, 64 * S, [[S, 1], [1, S]])
                P.emit("sync", lambda en, ap=drow_ap: en.dma_start(out=Dsp[:], in_=ap),
                       waits=[(s_ve, C_EV), (s_ve, recip_done[h - 1])], incs=[(s_dsc, 16)])
                c1 = s_dsc.count
                P.emit("vector", lambda en: en.reciprocal(Dspr[:], Dsp[:]),
                       waits=[(s_dsc, c1), (s_dsc, chain_out2[h - 1])], incs=[(s_ve, 1)])
                recip_done[h] = s_ve.count
                P.emit("sync", lambda en, ap=drow_ap: en.dma_start(out=ap, in_=Dspr[:]),
                       waits=[(s_ve, recip_done[h]), (s_pe, last_rep_pe[h - 1])],
                       incs=[(s_dsc, 16)])
                chain_out2[h] = s_dsc.count
                chain_done[h] = s_dsc.count
                for b in (1, 2, 0, 3):
                    pending_norm.append((h, b))


        for st_i in range(NSTEP + DEPTH):
            if st_i < NSTEP:
                emit_front(st_i)
            if st_i >= DEPTH:
                emit_back(st_i - DEPTH)
        while pending_norm:
            h2, b2 = pending_norm.pop(0)
            emit_norm_job(h2, b2)
        C_ATTN = s_ve.count
        C_DBG_P2 = 0
        if debug:
            P.emit("gpsimd", lambda en: en.dma_start(out=dbg["d_attnT"], in_=attnT[:]),
                   waits=[(s_ve, C_ATTN)], incs=[(s_vg, 16)])
            C_DBG_P2 = s_vg.count

        P.emit("sync", lambda en: en.dma_start(out=biasr, in_=biasr_d),
               waits=[(s_ve, C_ATTN)], incs=[(s_ldc, 16)])
        C_BIAS = s_ldc.count
        fbank_free = {0: 0, 1: 0}
        for f in range(6, NF):
            banks = [psum_rot[(f % 2) * 4 + j] for j in range(4)]
            for cc in range(NF):
                lhsT = wf8[:, cc * H + f * 128: cc * H + (f + 1) * 128]
                for sc in range(4):
                    wv = []
                    if cc == 0 and sc == 0:
                        wv = [(s_ldw, CW_F), (s_ve, C_ATTN)]
                        if debug and C_DBG_P2 > 0:
                            wv.append((s_vg, C_DBG_P2))
                        if fbank_free[f % 2] > 0:
                            wv.append((s_ac, fbank_free[f % 2]))
                    P.emit("tensor", lambda en, b=banks[sc], l=lhsT, cc=cc, sc=sc: en.matmul(
                        b[:], l, xqT[:, cc * S + sc * 512: cc * S + (sc + 1) * 512],
                        start=(cc == 0), stop=(cc == NF - 1)),
                        waits=wv,
                        incs=[(s_pe, 1)] if (cc == NF - 1 and sc == 3) else ())
            C_FG = s_pe.count
            for sc in range(4):
                dst = ffn16[:, f * S + sc * 512: f * S + (sc + 1) * 512]
                P.emit("scalar", lambda en, b=banks[sc], d=dst: en.copy(d, b[:]),
                       waits=[(s_pe, C_FG)] if sc == 0 else (), incs=[(s_ac, 1)])
            fbank_free[f % 2] = s_ac.count
        C_FFN = s_ac.count
        if debug:
            P.emit("gpsimd", lambda en: en.dma_start(out=dbg["d_ffn"], in_=ffn16[:, 0:NF * S]),
                   waits=[(s_ac, C_FFN)], incs=[(s_vg, 16)])

        if debug:
            P.emit("gpsimd", lambda en: en.dma_start(out=dbg["d_ffn"], in_=ffn16[:, 0:NF * S]),
                   waits=[(s_ac, C_FFN)], incs=[(s_vg, 16)])

        wo_bank_free = {}
        xre_used = {}
        tok_free = {}
        out_used = {}
        for sc in range(4):
            ot_slab = otbf[:, (sc % 2) * 4096:(sc % 2) * 4096 + 4096]
            for wave in range(2):
                for oo in range(4):
                    o = wave * 4 + oo
                    for hc in range(NF):
                        lhsT = wbfB[:, hc * H + o * 128: hc * H + (o + 1) * 128]
                        wv = []
                        if hc == 0:
                            wv = [(s_ldw, CW_O), (s_ve, C_ATTN), (s_ac, C_FFN)]
                            if wo_bank_free.get(oo, 0) > 0:
                                wv.append((s_ac, wo_bank_free[oo]))
                        P.emit("tensor", lambda en, b=pb[oo], l=lhsT, hc=hc, sc=sc: en.matmul(
                            b[:], l, attnT[:, hc * S + sc * 512: hc * S + (sc + 1) * 512],
                            start=(hc == 0), stop=(hc == NF - 1)),
                            waits=wv,
                            incs=[(s_pe, 1)] if hc == NF - 1 else ())
                    C_WO = s_pe.count
                    dsl = ot_slab[:, o * 512:(o + 1) * 512]
                    P.emit("scalar", lambda en, b=pb[oo], d=dsl: en.copy(d, b[:]),
                           waits=[(s_pe, C_WO)], incs=[(s_ac, 1)])
                    wo_bank_free[oo] = s_ac.count
            C_OT = s_ac.count

            for i in range(4):
                st_ = sc * 4 + i
                xsl_i = st_ % 4
                wv = [(s_ldc, C_BIAS), (s_ac, C_FFN)]
                if xre_used.get(xsl_i, 0) > 0:
                    wv.append((s_ve, xre_used[xsl_i]))
                P.emit("vector", lambda en, xs=xsl_i: en.tensor_copy(
                    x_re[:, xs * 1024:(xs + 1) * 1024], biasr),
                    waits=wv, incs=[(s_ve, 1)])
                C_PRE = s_ve.count
                P.emit("gpsimd", lambda en, st_=st_, xs=xsl_i: en.dma_start(
                    out=x_re[:, xs * 1024:(xs + 1) * 1024],
                    in_=x_d[st_ * 128:(st_ + 1) * 128, :], accum_op=ALU.add),
                    waits=[(s_ve, C_PRE)], incs=[(s_xr, 16)])
                C_XR = s_xr.count

                pair = [ps[0], ps[1]] if i % 2 == 0 else [ps[2], pr]
                wv_t = [(s_ac, C_OT)]
                if tok_free.get(i % 2, 0) > 0:
                    wv_t.append((s_ve, tok_free[i % 2]))
                for f in range(NF):
                    pt = pair[f // 4]
                    P.emit("tensor", lambda en, pt=pt, f=f, i=i, ot=ot_slab: en.matmul(
                        pt[:, (f % 4) * 128:(f % 4 + 1) * 128],
                        ot[:, f * 512 + i * 128: f * 512 + i * 128 + 128],
                        idbf[:], start=True, stop=False),
                        waits=wv_t if f == 0 else ())
                    P.emit("tensor", lambda en, pt=pt, f=f, sc=sc, i=i: en.matmul(
                        pt[:, (f % 4) * 128:(f % 4 + 1) * 128],
                        ffn16[:, f * S + sc * 512 + i * 128: f * S + sc * 512 + i * 128 + 128],
                        idf16[:], start=False, stop=True),
                        incs=[(s_pe, 1)] if f == NF - 1 else ())
                C_TOK = s_pe.count

                oslot = st_ % 2
                wv_f = [(s_pe, C_TOK), (s_xr, C_XR)]
                if out_used.get(oslot, 0) > 0:
                    wv_f.append((s_st, out_used[oslot]))
                for hlf in range(2):
                    P.emit("vector", lambda en, os_=oslot, xs=xsl_i, hh=hlf, pt=pair[hlf]: en.tensor_add(
                        out_f32[:, os_ * 1024 + hh * 512: os_ * 1024 + (hh + 1) * 512],
                        x_re[:, xs * 1024 + hh * 512: xs * 1024 + (hh + 1) * 512],
                        pt[:, 0:512]),
                        waits=wv_f if hlf == 0 else (), incs=[(s_ve, 1)])
                xre_used[xsl_i] = s_ve.count
                tok_free[i % 2] = s_ve.count
                C_OUT = s_ve.count
                P.emit("sync", lambda en, st_=st_, os_=oslot: en.dma_start(
                    out=y_d[st_ * 128:(st_ + 1) * 128, :],
                    in_=out_f32[:, os_ * 1024:(os_ + 1) * 1024]),
                    waits=[(s_ve, C_OUT)], incs=[(s_st, 16)])
                out_used[oslot] = s_st.count

        P.emit("sync", lambda en: en.nop(), waits=[(s_st, s_st.count)])
        P.finalize()
    return nc


def _host_inputs(L0_attn_x, L0_attn_Wq, L0_attn_Wk, L0_attn_Wv, L0_attn_Wo,
                 L0_ffn_weight, L0_ffn_bias):
    x = np.ascontiguousarray(np.asarray(L0_attn_x, dtype=np.float32))
    common = dict(
        wqT=np.ascontiguousarray(np.asarray(L0_attn_Wq, np.float32).T),
        wkT=np.ascontiguousarray(np.asarray(L0_attn_Wk, np.float32).T),
        wvT=np.ascontiguousarray(np.asarray(L0_attn_Wv, np.float32).T),
        woT=np.ascontiguousarray(np.asarray(L0_attn_Wo, np.float32).T),
        wfT=np.ascontiguousarray(np.asarray(L0_ffn_weight, np.float32).T),
        biasr=np.ascontiguousarray(np.broadcast_to(
            np.asarray(L0_ffn_bias, np.float32)[None, :], (128, H))),
        masks=_tri_masks(),
        idbf=np.eye(128, dtype=ml_dtypes.bfloat16),
        id8=np.eye(128, dtype=ml_dtypes.float8_e4m3),
        idf16=np.eye(128, dtype=np.float16),
        onesbf=np.ones((128, 128), dtype=ml_dtypes.bfloat16),
    )
    return x, common


def kernel(L0_attn_x, L0_attn_Wq, L0_attn_Wk, L0_attn_Wv, L0_attn_Wo,
           L0_ffn_weight, L0_ffn_bias):
    x, common = _host_inputs(L0_attn_x, L0_attn_Wq, L0_attn_Wk, L0_attn_Wv,
                             L0_attn_Wo, L0_ffn_weight, L0_ffn_bias)
    nc = build()
    in_maps = [dict(x=np.ascontiguousarray(x[b]), **common) for b in range(8)]
    res = run_bass_kernel_spmd(nc, in_maps, core_ids=list(range(8)))
    return np.stack([r["y"] for r in res.results], axis=0)


# version 10
# speedup vs baseline: 1.0576x; 1.0576x over previous
import numpy as np
import ml_dtypes
import concourse.bass as bass
import concourse.mybir as mybir
from concourse.bass_utils import run_bass_kernel_spmd
from contextlib import ExitStack

F32 = mybir.dt.float32
BF16 = mybir.dt.bfloat16
FP16 = mybir.dt.float16
FP8 = mybir.dt.float8e4
AF = mybir.ActivationFunctionType
ALU = mybir.AluOpType

S = 2048
H = 1024
NH = 16
NT = 16
NF = 8
SCALE = 64.0 ** -0.5
EPS = 1e-5

ENGS = ("tensor", "vector", "scalar", "sync", "gpsimd")


class Sem:
    def __init__(self, handle):
        self.h = handle
        self.count = 0


class Prog:

    def __init__(self, nc, ctx):
        self.nc = nc
        self.ctx = ctx
        self.streams = {e: [] for e in ENGS}
        self.sems = {}
        self._last_wait = {}

    def sem(self, name):
        if name not in self.sems:
            self.sems[name] = Sem(self.ctx.enter_context(self.nc.semaphore(name)))
        return self.sems[name]

    def emit(self, eng, fn, waits=(), incs=()):
        wl = []
        for s, v in waits:
            key = (eng, s.h.name)
            if v > 0 and self._last_wait.get(key, -1) < v:
                self._last_wait[key] = v
                wl.append((s, v))
        for s, k in incs:
            s.count += k
        self.streams[eng].append((fn, wl, list(incs)))

    def finalize(self):
        block = self.ctx.enter_context(self.nc.Block())
        streams = self.streams

        def runner(name):
            def _run(eng):
                with self.nc.allow_low_precision(
                        reason="intentional bf16 stores; accumulation stays fp32"):
                    for fn, waits, incs in streams[name]:
                        for s, v in waits:
                            eng.wait_ge(s.h, v)
                        inst = fn(eng)
                        for s, k in incs:
                            inst.then_inc(s.h, k)
            return _run

        block.tensor(runner("tensor"))
        block.vector(runner("vector"))
        block.scalar(runner("scalar"))
        block.sync(runner("sync"))
        block.gpsimd(runner("gpsimd"))


def _tri_masks():
    tl = np.arange(128)[:, None]
    sl = np.arange(128)[None, :]
    tri_hi = (tl <= sl).astype(np.float32)
    tri_lo = (tl >= sl).astype(np.float32)
    tri_lo_r0 = np.maximum(tri_lo, np.broadcast_to((tl == 0), (128, 128)).astype(np.float32))
    tri_hi_c0 = np.maximum(tri_hi, np.broadcast_to((sl == 0), (128, 128)).astype(np.float32))
    m = np.zeros((128, 1024), np.float32)
    m[:, 0:128] = tri_hi
    m[:, 128:256] = tri_lo
    m[:, 256:384] = tri_hi_c0
    m[:, 384:512] = tri_lo
    m[:, 512:640] = tri_lo_r0
    m[:, 640:768] = tri_hi
    return m.astype(ml_dtypes.bfloat16)


def build(debug=False, phase_limit=None):
    nc = bass.Bass("TRN2", target_bir_lowering=False)

    x_d = nc.dram_tensor("x", [S, H], F32, kind="ExternalInput").ap()
    wqT_d = nc.dram_tensor("wqT", [H, H], F32, kind="ExternalInput").ap()
    wkT_d = nc.dram_tensor("wkT", [H, H], F32, kind="ExternalInput").ap()
    wvT_d = nc.dram_tensor("wvT", [H, H], F32, kind="ExternalInput").ap()
    woT_d = nc.dram_tensor("woT", [H, H], F32, kind="ExternalInput").ap()
    wfT_d = nc.dram_tensor("wfT", [H, H], F32, kind="ExternalInput").ap()
    biasr_d = nc.dram_tensor("biasr", [128, H], F32, kind="ExternalInput").ap()
    masks_d = nc.dram_tensor("masks", [128, 1024], BF16, kind="ExternalInput").ap()
    idbf_d = nc.dram_tensor("idbf", [128, 128], BF16, kind="ExternalInput").ap()
    id8_d = nc.dram_tensor("id8", [128, 128], FP8, kind="ExternalInput").ap()
    idf16_d = nc.dram_tensor("idf16", [128, 128], FP16, kind="ExternalInput").ap()
    onesbf_d = nc.dram_tensor("onesbf", [128, 128], BF16, kind="ExternalInput").ap()
    y_d = nc.dram_tensor("y", [S, H], F32, kind="ExternalOutput").ap()

    dbg = {}
    if debug:
        for name, shape, dt in [
            ("d_xnT", [128, NF * S], BF16), ("d_xqT", [128, NF * S], FP8),
            ("d_qT", [128, NF * S], BF16), ("d_kT", [128, NF * S], BF16),
            ("d_v", [128, NT * 1040], BF16), ("d_attnT", [128, NF * S], BF16),
            ("d_ffn", [128, NF * S], FP16), ("d_e0", [97, 4 * S], BF16),
            ("d_e0q", [128, 256], BF16), ("d_stats", [128, 6 * NT], F32),
        ]:
            dbg[name] = nc.dram_tensor(name, shape, dt, kind="ExternalOutput").ap()

    with ExitStack() as ctx:
        e = ctx.enter_context
        xnT = e(nc.sbuf_tensor("xnT_attnT", [128, NF * S], BF16))
        attnT = xnT
        qtA = e(nc.sbuf_tensor("qt_ffn", [128, NF * S], BF16))
        QT = qtA
        ffn16 = qtA.bitcast(FP16)
        ktA = e(nc.sbuf_tensor("kt_xre", [128, NF * S], BF16))
        KT = ktA
        x_re = ktA.bitcast(F32)
        vA = e(nc.sbuf_tensor("v_out", [128, NT * 1040], BF16))
        V = vA
        out_f32 = vA.bitcast(F32)
        wbfA = e(nc.sbuf_tensor("wbf_a", [128, NF * H], BF16))
        E0 = wbfA
        wbfB = e(nc.sbuf_tensor("wbf_b", [128, NF * H], BF16))
        wf8 = e(nc.sbuf_tensor("wf8", [128, NF * H], FP8))
        xqT = e(nc.sbuf_tensor("xqT", [128, NF * S], FP8))
        otbf = xqT.bitcast(BF16)

        masks = e(nc.sbuf_tensor("sb_masks", [128, 768], BF16))
        idbf = e(nc.sbuf_tensor("sb_idbf", [128, 128], BF16))
        id8 = e(nc.sbuf_tensor("sb_id8", [128, 128], FP8))
        idf16 = e(nc.sbuf_tensor("sb_idf16", [128, 128], FP16))
        onesbf = e(nc.sbuf_tensor("sb_onesbf", [128, 128], BF16))
        Gc = e(nc.sbuf_tensor("Gc", [128, 8 * 97], BF16))
        Gq = e(nc.sbuf_tensor("Gq", [128, 8 * 16], BF16))
        Vg = e(nc.sbuf_tensor("Vg", [97, 1040], BF16))
        E0qT = e(nc.sbuf_tensor("E0qT", [128, 256], BF16))
        Esb = e(nc.sbuf_tensor("Esb", [128, 6 * 448], BF16))
        scA = e(nc.sbuf_tensor("scA", [128, S], BF16))
        Drow = e(nc.sbuf_tensor("Drow", [65, S], BF16))
        Dsp = e(nc.sbuf_tensor("Dsp", [128, 16], BF16))
        Dspr = e(nc.sbuf_tensor("Dspr", [128, 16], BF16))
        rc = e(nc.sbuf_tensor("rc", [128, 2 * 512], BF16))
        sumc = e(nc.sbuf_tensor("sumc", [128, NT], F32))
        cinvH = e(nc.sbuf_tensor("cinvH", [128, 4], F32))
        cnegH = e(nc.sbuf_tensor("cnegH", [128, 4], F32))
        cepsm = e(nc.sbuf_tensor("cepsm", [128, 4], F32))
        sqc = e(nc.sbuf_tensor("sqc", [128, NT], F32))
        zb = e(nc.sbuf_tensor("zb", [128, 512], BF16))
        mcol = e(nc.sbuf_tensor("mcol", [128, NT], F32))
        vcol = e(nc.sbuf_tensor("vcol", [128, NT], F32))
        tcol = sumc
        rstd = vcol
        nmr = tcol

        pb = [e(nc.psum_tensor(f"pb{i}", [128, 512], F32)) for i in range(4)]
        ps = [e(nc.psum_tensor(f"ps{i}", [128, 512], F32)) for i in range(3)]
        pr = e(nc.psum_tensor("pr", [128, 512], F32))
        psum_rot = pb + ps + [pr]

        P = Prog(nc, ctx)
        s_ldc = P.sem("ldc")
        s_ldx = P.sem("ldx")
        s_ldw = P.sem("ldw")
        s_pe = P.sem("pe")
        s_ve = P.sem("ve")
        s_ac = P.sem("ac")
        s_gp = P.sem("gp")
        s_st = P.sem("st")
        s_vg = P.sem("vg")
        s_xr = P.sem("xr")
        s_dsc = P.sem("dsc")

        biasr = ktA.bitcast(F32)[:, 4096:5120]
        for dst, src in [(masks, masks_d[:, 0:768]), (idbf, idbf_d), (id8, id8_d),
                         (idf16, idf16_d), (onesbf, onesbf_d)]:
            P.emit("sync", lambda en, d=dst, s=src: en.dma_start(out=d[:], in_=s),
                   incs=[(s_ldc, 16)])
        C_CONST = s_ldc.count

        def wload(dst, src):
            return lambda en: en.dma_start(
                out=dst[:].rearrange("p (ic o) -> p ic o", o=H),
                in_=src.rearrange("(ic p) o -> p ic o", p=128))

        P.emit("gpsimd", wload(wbfA, wqT_d), incs=[(s_ldw, 16)])
        CW_Q = s_ldw.count
        P.emit("gpsimd", wload(wbfB, wkT_d), incs=[(s_ldw, 16)])
        CW_K = s_ldw.count
        P.emit("gpsimd", wload(wf8, wfT_d), incs=[(s_ldw, 16)])
        CW_F = s_ldw.count

        x_sb = qtA.bitcast(F32)
        xn_tok = ktA
        xq_tok = ktA.bitcast(FP8)
        xd3 = x_d.rearrange("(c t p) h -> c p t h", p=128, t=4)

        rot_i = 0
        rot_free = [0] * 8
        chunk_done = {}
        P.emit("vector", lambda en: en.memset(cinvH[:], 1.0 / H))
        P.emit("vector", lambda en: en.memset(cnegH[:], -1.0 / H))
        P.emit("vector", lambda en: en.memset(cepsm[:], float(EPS)), incs=[(s_ve, 1)])
        C_MEMS = s_ve.count

        kt_f32 = ktA.bitcast(F32)

        def xslot(c):
            if c == 2:
                return kt_f32[:, 4096:8192]
            sl = 0 if c == 3 else c
            return x_sb[:, sl * 4096:(sl + 1) * 4096]

        xdma_count = {}
        for c in range(3):
            P.emit("sync", lambda en, c=c: en.dma_start(
                out=xslot(c).rearrange("p (t h) -> p t h", h=1024), in_=xd3[c]),
                incs=[(s_ldx, 16)])
            xdma_count[c] = s_ldx.count

        for c in range(4):
            if c == 1:
                P.emit("sync", lambda en: en.dma_start(
                    out=xslot(3).rearrange("p (t h) -> p t h", h=1024), in_=xd3[3]),
                    waits=[(s_ve, chunk_done[0])], incs=[(s_ldx, 16)])
                xdma_count[3] = s_ldx.count
            C_X = xdma_count[c]
            xsq = ktA.bitcast(F32)
            for i in range(4):
                t = 4 * c + i
                xt = xslot(c)[:, i * 1024:(i + 1) * 1024]
                P.emit("vector", lambda en, xt=xt, t=t: en.reduce_sum(
                    sumc[:, t:t + 1], xt, axis=mybir.AxisListType.X),
                    waits=[(s_ldx, C_X)])
                P.emit("scalar", lambda en, xt=xt: en.activation(
                    xsq[:, 3072:4096], xt, AF.Square),
                    waits=[(s_ldx, C_X)] if i == 0 else [(s_ve, s_ve.count)],
                    incs=[(s_ac, 1)])
                C_SQ = s_ac.count
                P.emit("vector", lambda en, t=t: en.reduce_sum(
                    sqc[:, t:t + 1], xsq[:, 3072:4096], axis=mybir.AxisListType.X),
                    waits=[(s_ac, C_SQ)], incs=[(s_ve, 1)])
            C_RED = s_ve.count
            t0, t1 = 4 * c, 4 * c + 4
            sl = slice(t0, t1)
            P.emit("gpsimd", lambda en, sl=sl: en.tensor_mul(mcol[:, sl], sumc[:, sl], cnegH[:, 0:4]),
                   waits=[(s_ve, C_RED), (s_ve, C_MEMS)])
            P.emit("gpsimd", lambda en, sl=sl: en.tensor_mul(tcol[:, sl], mcol[:, sl], mcol[:, sl]))
            P.emit("gpsimd", lambda en, sl=sl: en.tensor_mul(vcol[:, sl], sqc[:, sl], cinvH[:, 0:4]))
            P.emit("gpsimd", lambda en, sl=sl: en.tensor_add(vcol[:, sl], vcol[:, sl], cepsm[:, 0:4]))
            P.emit("gpsimd", lambda en, sl=sl: en.tensor_sub(vcol[:, sl], vcol[:, sl], tcol[:, sl]),
                   incs=[(s_gp, 1)])
            C_V = s_gp.count
            P.emit("vector", lambda en, sl=sl: en.reciprocal(tcol[:, sl], vcol[:, sl]),
                   waits=[(s_gp, C_V)], incs=[(s_ve, 1)])
            C_RV = s_ve.count
            P.emit("scalar", lambda en, sl=sl: en.activation(rstd[:, sl], tcol[:, sl], AF.Sqrt),
                   waits=[(s_ve, C_RV)], incs=[(s_ac, 1)])
            C_RS = s_ac.count
            P.emit("gpsimd", lambda en, sl=sl: en.tensor_mul(nmr[:, sl], mcol[:, sl], rstd[:, sl]),
                   waits=[(s_ac, C_RS)], incs=[(s_gp, 1)])
            C_NMR = s_gp.count

            for i in range(4):
                t = 4 * c + i
                xt = xslot(c)[:, i * 1024:(i + 1) * 1024]
                P.emit("scalar", lambda en, xt=xt, i=i, t=t: en.activation(
                    xn_tok[:, i * 1024:(i + 1) * 1024], xt, AF.Identity,
                    scale=rstd[:, t:t + 1], bias=nmr[:, t:t + 1]),
                    waits=[(s_gp, C_NMR)])
                P.emit("scalar", lambda en, xt=xt, i=i, t=t: en.activation(
                    xq_tok[:, 8192 + i * 1024: 8192 + (i + 1) * 1024], xt, AF.Identity,
                    scale=rstd[:, t:t + 1], bias=nmr[:, t:t + 1]),
                    incs=[(s_ac, 1)])
            C_LN = s_ac.count

            for kind in range(2):
                for f in range(NF):
                    bi = rot_i % 8
                    bank = psum_rot[bi]
                    for i in range(4):
                        wv = []
                        if i == 0:
                            wv = [(s_ac, C_LN), (s_ldc, C_CONST)]
                            if rot_free[bi] > 0:
                                wv.append((s_ve, rot_free[bi]))
                        if kind == 0:
                            lhsT = xn_tok[:, i * 1024 + f * 128:i * 1024 + (f + 1) * 128]
                            rhs = idbf[:]
                        else:
                            lhsT = xq_tok[:, 8192 + i * 1024 + f * 128: 8192 + i * 1024 + (f + 1) * 128]
                            rhs = id8[:]
                        P.emit("tensor", lambda en, b=bank, i=i, l=lhsT, r=rhs: en.matmul(
                            b[:, i * 128:(i + 1) * 128], l, r, start=True, stop=True),
                            waits=wv, incs=[(s_pe, 1)] if i == 3 else ())
                    C_T = s_pe.count
                    dst = (xnT if kind == 0 else xqT)[:, f * S + c * 512: f * S + (c + 1) * 512]
                    P.emit("vector", lambda en, b=bank, d=dst: en.tensor_copy(d, b[:]),
                           waits=[(s_pe, C_T)], incs=[(s_ve, 1)])
                    rot_free[bi] = s_ve.count
                    rot_i += 1
            chunk_done[c] = s_ve.count
        C_P0 = s_ve.count
        C_DBG_P0 = 0
        if debug:
            def dump_stats(en):
                return en.dma_start(out=dbg["d_stats"].rearrange("p (k t) -> p k t", t=NT),
                                    in_=bass.AP(sumc, 0, [[NT, 128], [0, 1], [1, NT]]))
            for ii, tens in enumerate([sumc, sqc, mcol, vcol, rstd, nmr]):
                P.emit("gpsimd", lambda en, ii=ii, tn=tens: en.dma_start(
                    out=dbg["d_stats"][:, ii * NT:(ii + 1) * NT], in_=tn[:]),
                    waits=[(s_ve, C_P0)], incs=[(s_vg, 16)])
            P.emit("gpsimd", lambda en: en.dma_start(out=dbg["d_xnT"], in_=xnT[:]),
                   waits=[(s_ve, C_P0)], incs=[(s_vg, 16)])
            P.emit("gpsimd", lambda en: en.dma_start(out=dbg["d_xqT"], in_=xqT[:]),
                   incs=[(s_vg, 16)])
            C_DBG_P0 = s_vg.count

        if phase_limit == "p0":
            P.emit("sync", lambda en: en.nop(), waits=[(s_vg, s_vg.count)])
            P.finalize()
            return nc

        bank_free = {0: 0, 1: 0}

        def gemm_T(wsb, out_sb, wait_w):
            for o in range(NF):
                banks = [psum_rot[(o % 2) * 4 + j] for j in range(4)]
                for ic in range(NF):
                    lhsT = wsb[:, ic * H + o * 128: ic * H + (o + 1) * 128]
                    for sc in range(4):
                        wv = []
                        if ic == 0 and sc == 0:
                            wv = [(s_ldw, wait_w), (s_ve, C_P0)]
                            if bank_free[o % 2] > 0:
                                wv.append((s_ve, bank_free[o % 2]))
                        P.emit("tensor", lambda en, b=banks[sc], l=lhsT, sc=sc, ic=ic: en.matmul(
                            b[:], l, xnT[:, ic * S + sc * 512: ic * S + (sc + 1) * 512],
                            start=(ic == 0), stop=(ic == NF - 1)),
                            waits=wv,
                            incs=[(s_pe, 1)] if (ic == NF - 1 and sc == 3) else ())
                C_G = s_pe.count
                for sc in range(4):
                    dst = out_sb[:, o * S + sc * 512: o * S + (sc + 1) * 512]
                    P.emit("vector", lambda en, b=banks[sc], d=dst: en.tensor_copy(d, b[:]),
                           waits=[(s_pe, C_G)] if sc == 0 else (), incs=[(s_ve, 1)])
                bank_free[o % 2] = s_ve.count
            return s_ve.count

        C_QT = gemm_T(wbfA, QT, CW_Q)
        P.emit("gpsimd", wload(wbfA, wvT_d),
               waits=[(s_pe, s_pe.count)], incs=[(s_ldw, 16)])
        CW_V = s_ldw.count
        C_KT = gemm_T(wbfB, KT, CW_K)
        P.emit("gpsimd", wload(wbfB, woT_d),
               waits=[(s_pe, s_pe.count)], incs=[(s_ldw, 16)])
        CW_O = s_ldw.count

        P.emit("vector", lambda en: en.memset(V[:], 1.0))
        P.emit("vector", lambda en: en.memset(zb[:], 0.0), incs=[(s_ve, 1)])
        C_VONE = s_ve.count
        vbank_free = {0: 0, 1: 0}
        C_V0 = 0
        for t in range(NT):
            banks = [psum_rot[(t % 2) * 2 + j] for j in range(2)]
            for ic in range(NF):
                lhsT = xnT[:, ic * S + t * 128: ic * S + (t + 1) * 128]
                for oc in range(2):
                    wv = []
                    if ic == 0 and oc == 0:
                        wv = [(s_ldw, CW_V), (s_ve, C_VONE), (s_ve, C_KT)]
                        if vbank_free[t % 2] > 0:
                            wv.append((s_ve, vbank_free[t % 2]))
                    P.emit("tensor", lambda en, b=banks[oc], l=lhsT, ic=ic, oc=oc: en.matmul(
                        b[:], l, wbfA[:, ic * H + oc * 512: ic * H + (oc + 1) * 512],
                        start=(ic == 0), stop=(ic == NF - 1)),
                        waits=wv,
                        incs=[(s_pe, 1)] if (ic == NF - 1 and oc == 1) else ())
            C_VG = s_pe.count
            for oc in range(2):
                dstap = V[:, t * 1040 + oc * 520: t * 1040 + (oc + 1) * 520].rearrange(
                    "p (h d) -> p h d", d=65)[:, :, 0:64]
                srcap = banks[oc][:].rearrange("p (h d) -> p h d", d=64)
                zap = zb[:].rearrange("p (h d) -> p h d", d=64)
                P.emit("vector", lambda en, sp=srcap, dp=dstap, zp=zap: en.tensor_tensor(dp, sp, zp, ALU.add),
                       waits=[(s_pe, C_VG)] if oc == 0 else (), incs=[(s_ve, 1)])
            vbank_free[t % 2] = s_ve.count
            if t == 0:
                C_V0 = s_ve.count
        C_VALL = s_ve.count
        C_DBG_P1 = 0
        if debug:
            P.emit("gpsimd", lambda en: en.dma_start(out=dbg["d_qT"], in_=QT[:]),
                   waits=[(s_ve, C_VALL)], incs=[(s_vg, 16)])
            P.emit("gpsimd", lambda en: en.dma_start(out=dbg["d_kT"], in_=KT[:]),
                   incs=[(s_vg, 16)])
            P.emit("gpsimd", lambda en: en.dma_start(out=dbg["d_v"], in_=V[:]),
                   incs=[(s_vg, 16)])
            C_DBG_P1 = s_vg.count

        for j in range(4):
            P.emit("gpsimd", lambda en, j=j: en.dma_start(out=Vg[32 * j:32 * j + 1, :], in_=V[0:1, 0:1040]),
                   waits=[(s_ve, C_V0)] if j == 0 else (), incs=[(s_vg, 16)])
        C_VGC = s_vg.count

        P.emit("vector", lambda en: en.memset(Gc[:], 0.0), waits=[(s_ve, C_KT)])
        P.emit("vector", lambda en: en.memset(Gq[:], 0.0))
        for cch in range(8):
            colA = 32 * ((2 * cch) % 4)
            colB = 32 * ((2 * cch + 1) % 4)
            P.emit("vector", lambda en, c=cch, col=colA: en.tensor_copy(
                Gc[0:64, c * 97 + col: c * 97 + col + 1], KT[0:64, c * S: c * S + 1]))
            P.emit("vector", lambda en, c=cch, col=colB: en.tensor_copy(
                Gc[64:128, c * 97 + col: c * 97 + col + 1], KT[64:128, c * S: c * S + 1]))
            P.emit("vector", lambda en, c=cch: en.tensor_copy(
                Gq[0:64, c * 16 + 2 * c: c * 16 + 2 * c + 1], QT[0:64, c * S: c * S + 1]),
                waits=[(s_ve, C_QT)] if cch == 0 else ())
            P.emit("vector", lambda en, c=cch: en.tensor_copy(
                Gq[64:128, c * 16 + 2 * c + 1: c * 16 + 2 * c + 2], QT[64:128, c * S: c * S + 1]),
                incs=[(s_ve, 1)] if cch == 7 else ())
        C_GBUILD = s_ve.count

        for tt in range(NT):
            for cch in range(NF):
                wv = []
                if tt == 0 and cch == 0:
                    wv = [(s_ve, C_GBUILD), (s_ve, C_VALL)]
                P.emit("tensor", lambda en, tt=tt, c=cch: en.matmul(
                    ps[0][:, tt * 16:(tt + 1) * 16],
                    KT[:, c * S + tt * 128: c * S + (tt + 1) * 128],
                    Gq[:, c * 16:(c + 1) * 16],
                    start=(c == 0), stop=(c == NF - 1)),
                    waits=wv,
                    incs=[(s_pe, 1)] if (tt == NT - 1 and cch == NF - 1) else ())
        C_S0Q = s_pe.count
        P.emit("scalar", lambda en: en.activation(E0qT[:], ps[0][:, 0:256], AF.Exp, scale=float(SCALE)),
               waits=[(s_pe, C_S0Q)], incs=[(s_ac, 1)])
        C_E0Q = s_ac.count

        e0_wait = {0: 0, 1: 0}
        for wv_i in range(4):
            for sc in range(4):
                k = wv_i * 4 + sc
                bank = pr if k % 2 == 0 else ps[1]
                wvl = [(s_ve, C_GBUILD)]
                if e0_wait[k % 2] > 0:
                    wvl.append((s_ac, e0_wait[k % 2]))
                for ci, cch in enumerate((2 * wv_i, 2 * wv_i + 1)):
                    P.emit("tensor", lambda en, c=cch, sc=sc, b=bank, ci=ci: en.matmul(
                        b[0:97, :], Gc[:, c * 97:(c + 1) * 97],
                        QT[:, c * S + sc * 512: c * S + (sc + 1) * 512],
                        start=(ci == 0), stop=(ci == 1)),
                        waits=wvl if ci == 0 else (), incs=[(s_pe, 1)] if ci == 1 else ())
                C_E0M = s_pe.count
                P.emit("scalar", lambda en, w=wv_i, sc=sc, b=bank: en.activation(
                    E0[0:97, w * S + sc * 512: w * S + (sc + 1) * 512], b[0:97, :],
                    AF.Exp, scale=float(SCALE)),
                    waits=[(s_pe, C_E0M)], incs=[(s_ac, 1)])
                e0_wait[k % 2] = s_ac.count
        C_E0 = s_ac.count
        if debug:
            P.emit("gpsimd", lambda en: en.dma_start(out=dbg["d_e0"], in_=E0[0:97, 0:4 * S]),
                   waits=[(s_ac, C_E0)], incs=[(s_vg, 16)])
            P.emit("gpsimd", lambda en: en.dma_start(out=dbg["d_e0q"], in_=E0qT[:]),
                   waits=[(s_ac, C_E0Q)], incs=[(s_vg, 16)])

        ps_free = [C_E0Q, e0_wait[1], 0]
        ebuf_free = [0, 0, 0]
        pb_free = [0, 0, 0, 0]
        pr_free = [0]
        pending_norm = []
        last_rep_pe = {-1: 0}
        chain_done = {}
        recip_done = {-1: 0}
        chain_out2 = {-1: 0}

        def emit_norm_job(h2, b2):
            hr2 = (h2 % 2) * 64
            hf2 = h2 // 2
            wv = [(s_dsc, chain_done[h2]), (s_ldc, C_CONST), (s_ac, C_E0)]
            if pr_free[0] != 0:
                wv.append((s_ac, pr_free[0][1]))
            P.emit("tensor", lambda en, b=b2, po=hr2: en.matmul(
                pr[po:po + 64, :], onesbf[64:65, 0:64], Drow[64:65, b * 512:(b + 1) * 512],
                start=True, stop=True, tile_position=(64, po)),
                waits=wv, incs=[(s_pe, 1)])
            C_REP = s_pe.count
            last_rep_pe[h2] = C_REP
            P.emit("scalar", lambda en, b=b2, po=hr2: en.copy(
                rc[po:po + 64, (b % 2) * 512:(b % 2 + 1) * 512], pr[po:po + 64, :]),
                waits=[(s_pe, C_REP)], incs=[(s_ac, 1)])
            pr_free[0] = ("ac", s_ac.count)
            dst = attnT[hr2:hr2 + 64, hf2 * S + b2 * 512: hf2 * S + (b2 + 1) * 512]
            wv2 = [(s_ac, s_ac.count)]
            if debug and C_DBG_P0 > 0:
                wv2.append((s_vg, C_DBG_P0))
            P.emit("vector", lambda en, d=dst, b=b2, po=hr2: en.tensor_tensor(
                d, scA[po:po + 64, b * 512:(b + 1) * 512],
                rc[po:po + 64, (b % 2) * 512:(b % 2 + 1) * 512], ALU.mult),
                waits=wv2, incs=[(s_ve, 1)])

        NSTEP = NH * NT
        scores_head_done = {}
        DEPTH = 5
        sinfo = {}
        touched_h = {}
        ebuf_free = [0, 0, 0, 0, 0, 0]

        def emit_front(st):
            h, tc = divmod(st, NT)
            hr = (h % 2) * 64
            hf = h // 2
            g = st % 3
            slot = st % 6
            if tc == 0:
                q0, q1, ocol, ncol = 0, 2, 128, 256
            elif tc == NT - 1:
                q0, q1, ocol, ncol = tc - 1, tc + 1, 0, 256
            else:
                q0, q1, ocol, ncol = tc - 1, tc + 2, 0, 384
            wv = [(s_ve, C_VALL), (s_vg, C_VGC), (s_ac, C_E0)]
            if ps_free[g] > 0:
                wv.append((s_ac, ps_free[g]))
            lhsT = KT[hr:hr + 64, hf * S + tc * 128: hf * S + (tc + 1) * 128]
            rhs = QT[hr:hr + 64, hf * S + q0 * 128: hf * S + q1 * 128]
            P.emit("tensor", lambda en, b=ps[g], l=lhsT, r=rhs, oc=ocol, nn=ncol, hr=hr: en.matmul(
                b[:, oc:oc + nn], l, r, start=True, stop=True, tile_position=(hr, 0)),
                waits=wv, incs=[(s_pe, 1)])
            C_SC = s_pe.count
            if tc == NT - 1:
                scores_head_done[h] = C_SC

            eoff, ewid = (128, 256) if tc == 0 else ((0, 256) if tc == NT - 1 else (0, 384))
            wv = [(s_pe, C_SC)]
            if ebuf_free[slot] > 0:
                wv.append((s_pe, ebuf_free[slot]))
            P.emit("scalar", lambda en, b=ps[g], slot=slot, eo=eoff, ew=ewid: en.activation(
                Esb[:, slot * 448 + eo: slot * 448 + eo + ew], b[:, eo:eo + ew],
                AF.Exp, scale=float(SCALE)),
                waits=wv, incs=[(s_ac, 1)])
            ps_free[g] = s_ac.count
            C_EXP = s_ac.count

            eng = "vector" if st % 2 == 0 else "gpsimd"
            sem = s_ve if eng == "vector" else s_gp
            if tc == 0:
                eap = Esb[:, slot * 448 + 256: slot * 448 + 384]
                map_ = masks[:, 512:640]
            elif tc == 1:
                eap = Esb[:, slot * 448: slot * 448 + 384].rearrange("p (a b) -> p a b", b=128)[:, 0:3:2, :]
                map_ = masks[:, 256:512].rearrange("p (a b) -> p a b", b=128)
            elif tc == NT - 1:
                eap = Esb[:, slot * 448: slot * 448 + 128]
                map_ = masks[:, 640:768]
            else:
                eap = Esb[:, slot * 448: slot * 448 + 384].rearrange("p (a b) -> p a b", b=128)[:, 0:3:2, :]
                map_ = masks[:, 0:256].rearrange("p (a b) -> p a b", b=128)
            P.emit(eng, lambda en, eap=eap, map_=map_: en.tensor_tensor(eap, eap, map_, ALU.mult),
                   waits=[(s_ac, C_EXP), (s_ldc, C_CONST)], incs=[(sem, 1)])
            sinfo[st] = (h, tc, slot, sem, sem.count)

        def emit_back(st):
            h, tc, slot, mask_sem, mask_cnt = sinfo.pop(st)
            hr = (h % 2) * 64
            hf = h // 2
            er0 = 32 * (h % 4)
            wv0 = h // 4
            if tc == 0:
                touched_h[h] = [False, False, False, False]
            touched = touched_h[h]
            if tc == 0:
                blocks = [(128, 0), (256, 1)]
            elif tc == NT - 1:
                blocks = [(0, tc - 1), (128, tc)]
            else:
                blocks = [(0, tc - 1), (128, tc), (256, tc + 1)]
            runs = []
            for ecol, Tq in blocks:
                b = Tq // 4
                col = (Tq % 4) * 128
                fresh = (Tq == tc + 1)
                if (runs and not fresh and runs[-1][1] == b
                        and runs[-1][2] + runs[-1][3] == col and not runs[-1][4]):
                    runs[-1] = (runs[-1][0], b, runs[-1][2], runs[-1][3] + 128, False)
                elif (runs and tc == 0 and runs[-1][1] == b
                      and runs[-1][2] + runs[-1][3] == col):
                    runs[-1] = (runs[-1][0], b, runs[-1][2], runs[-1][3] + 128, True)
                else:
                    runs.append((ecol, b, col, 128, fresh))
            lhsT_v = V[:, tc * 1040 + h * 65: tc * 1040 + (h + 1) * 65]
            first = True
            for ecol, b, col, width, _fresh in runs:
                wv = []
                if first:
                    wv.append((mask_sem, mask_cnt))
                    first = False
                st_flag = not touched[b]
                if st_flag and pb_free[b] > 0:
                    wv.append((s_ve, pb_free[b]))
                touched[b] = True
                P.emit("tensor", lambda en, bb=pb[b], l=lhsT_v, slot=slot, ec=ecol, col=col, wd=width, sf=st_flag: en.matmul(
                    bb[0:65, col:col + wd], l,
                    Esb[:, slot * 448 + ec: slot * 448 + ec + wd],
                    start=sf, stop=False),
                    waits=wv, incs=[(s_pe, 1)])
            if tc >= 2:
                P.emit("tensor", lambda en, l=lhsT_v, tc=tc, h=h: en.matmul(
                    pb[0][0:65, 0:1], l, E0qT[:, tc * 16 + h: tc * 16 + h + 1],
                    start=False, stop=(tc == 15)),
                    waits=[(s_ac, C_E0Q)], incs=[(s_pe, 1)])
            e0b = []
            if tc == 4:
                e0b = [(0, 256, 256, False)]
            elif tc == 8:
                e0b = [(1, 0, 512, True)]
            elif tc == 12:
                e0b = [(2, 0, 512, True)]
            elif tc == 15:
                e0b = [(3, 0, 512, True)]
            for b, col, width, stp in e0b:
                P.emit("tensor", lambda en, b=b, col=col, wd=width, stp=stp, er=er0, w=wv0, h=h: en.matmul(
                    pb[b][0:65, col:col + wd],
                    Vg[er:er + 1, h * 65:(h + 1) * 65],
                    E0[er:er + 1, w * S + b * 512 + col: w * S + b * 512 + col + wd],
                    start=False, stop=stp, tile_position=(er, 0)),
                    incs=[(s_pe, 1)])
            ebuf_free[slot] = s_pe.count

            if tc in (7, 9, 11, 13) and pending_norm:
                h2, b2 = pending_norm.pop(0)
                emit_norm_job(h2, b2)


            jobs = []
            if tc == 8:
                jobs = [1]
            elif tc == 12:
                jobs = [2]
            elif tc == 15:
                jobs = [0, 3]
            for b in jobs:
                C_AVB = s_pe.count
                wv_e = [(s_pe, C_AVB)]
                if h > 0 and tc == 8:
                    wv_e.append((s_dsc, chain_done[h - 1]))
                P.emit("vector", lambda en, b=b, hr=hr: en.tensor_copy(
                    scA[hr:hr + 64, b * 512:(b + 1) * 512], pb[b][0:64, :]),
                    waits=wv_e)
                P.emit("vector", lambda en, b=b: en.tensor_copy(
                    Drow[64:65, b * 512:(b + 1) * 512], pb[b][64:65, :]),
                    incs=[(s_ve, 1)])
                pb_free[b] = s_ve.count
            if tc == 15:
                C_EV = s_ve.count
                drow_ap = bass.AP(Drow, 64 * S, [[S, 1], [1, S]])
                P.emit("sync", lambda en, ap=drow_ap: en.dma_start(out=Dsp[:], in_=ap),
                       waits=[(s_ve, C_EV), (s_ve, recip_done[h - 1])], incs=[(s_dsc, 16)])
                c1 = s_dsc.count
                P.emit("vector", lambda en: en.reciprocal(Dspr[:], Dsp[:]),
                       waits=[(s_dsc, c1), (s_dsc, chain_out2[h - 1])], incs=[(s_ve, 1)])
                recip_done[h] = s_ve.count
                P.emit("sync", lambda en, ap=drow_ap: en.dma_start(out=ap, in_=Dspr[:]),
                       waits=[(s_ve, recip_done[h]), (s_pe, last_rep_pe[h - 1])],
                       incs=[(s_dsc, 16)])
                chain_out2[h] = s_dsc.count
                chain_done[h] = s_dsc.count
                for b in (1, 2, 0, 3):
                    pending_norm.append((h, b))


        for st_i in range(NSTEP + DEPTH):
            if st_i < NSTEP:
                emit_front(st_i)
            if st_i >= DEPTH:
                emit_back(st_i - DEPTH)
        while pending_norm:
            h2, b2 = pending_norm.pop(0)
            emit_norm_job(h2, b2)
        C_ATTN = s_ve.count
        C_DBG_P2 = 0
        if debug:
            P.emit("gpsimd", lambda en: en.dma_start(out=dbg["d_attnT"], in_=attnT[:]),
                   waits=[(s_ve, C_ATTN)], incs=[(s_vg, 16)])
            C_DBG_P2 = s_vg.count

        P.emit("sync", lambda en: en.dma_start(out=biasr, in_=biasr_d),
               waits=[(s_ve, C_ATTN)], incs=[(s_ldc, 16)])
        C_BIAS = s_ldc.count
        fbank_free = {0: 0, 1: 0}
        for f in range(NF):
            banks = [psum_rot[(f % 2) * 4 + j] for j in range(4)]
            for cc in range(NF):
                lhsT = wf8[:, cc * H + f * 128: cc * H + (f + 1) * 128]
                for sc in range(4):
                    wv = []
                    if cc == 0 and sc == 0:
                        wv = [(s_ldw, CW_F), (s_ve, C_ATTN)]
                        if debug and C_DBG_P2 > 0:
                            wv.append((s_vg, C_DBG_P2))
                        if fbank_free[f % 2] > 0:
                            wv.append((s_ac, fbank_free[f % 2]))
                    P.emit("tensor", lambda en, b=banks[sc], l=lhsT, cc=cc, sc=sc: en.matmul(
                        b[:], l, xqT[:, cc * S + sc * 512: cc * S + (sc + 1) * 512],
                        start=(cc == 0), stop=(cc == NF - 1)),
                        waits=wv,
                        incs=[(s_pe, 1)] if (cc == NF - 1 and sc == 3) else ())
            C_FG = s_pe.count
            for sc in range(4):
                dst = ffn16[:, f * S + sc * 512: f * S + (sc + 1) * 512]
                P.emit("scalar", lambda en, b=banks[sc], d=dst: en.copy(d, b[:]),
                       waits=[(s_pe, C_FG)] if sc == 0 else (), incs=[(s_ac, 1)])
            fbank_free[f % 2] = s_ac.count
        C_FFN = s_ac.count
        if debug:
            P.emit("gpsimd", lambda en: en.dma_start(out=dbg["d_ffn"], in_=ffn16[:, 0:NF * S]),
                   waits=[(s_ac, C_FFN)], incs=[(s_vg, 16)])

        if debug:
            P.emit("gpsimd", lambda en: en.dma_start(out=dbg["d_ffn"], in_=ffn16[:, 0:NF * S]),
                   waits=[(s_ac, C_FFN)], incs=[(s_vg, 16)])

        wo_bank_free = {}
        xre_used = {}
        tok_free = {}
        out_used = {}
        for sc in range(4):
            ot_slab = otbf[:, (sc % 2) * 4096:(sc % 2) * 4096 + 4096]
            for wave in range(2):
                for oo in range(4):
                    o = wave * 4 + oo
                    for hc in range(NF):
                        lhsT = wbfB[:, hc * H + o * 128: hc * H + (o + 1) * 128]
                        wv = []
                        if hc == 0:
                            wv = [(s_ldw, CW_O), (s_ve, C_ATTN), (s_ac, C_FFN)]
                            if wo_bank_free.get(oo, 0) > 0:
                                wv.append((s_ac, wo_bank_free[oo]))
                        P.emit("tensor", lambda en, b=pb[oo], l=lhsT, hc=hc, sc=sc: en.matmul(
                            b[:], l, attnT[:, hc * S + sc * 512: hc * S + (sc + 1) * 512],
                            start=(hc == 0), stop=(hc == NF - 1)),
                            waits=wv,
                            incs=[(s_pe, 1)] if hc == NF - 1 else ())
                    C_WO = s_pe.count
                    dsl = ot_slab[:, o * 512:(o + 1) * 512]
                    P.emit("scalar", lambda en, b=pb[oo], d=dsl: en.copy(d, b[:]),
                           waits=[(s_pe, C_WO)], incs=[(s_ac, 1)])
                    wo_bank_free[oo] = s_ac.count
            C_OT = s_ac.count

            for i in range(4):
                st_ = sc * 4 + i
                xsl_i = st_ % 4
                wv = [(s_ldc, C_BIAS), (s_ac, C_FFN)]
                if xre_used.get(xsl_i, 0) > 0:
                    wv.append((s_ve, xre_used[xsl_i]))
                P.emit("vector", lambda en, xs=xsl_i: en.tensor_copy(
                    x_re[:, xs * 1024:(xs + 1) * 1024], biasr),
                    waits=wv, incs=[(s_ve, 1)])
                C_PRE = s_ve.count
                P.emit("gpsimd", lambda en, st_=st_, xs=xsl_i: en.dma_start(
                    out=x_re[:, xs * 1024:(xs + 1) * 1024],
                    in_=x_d[st_ * 128:(st_ + 1) * 128, :], accum_op=ALU.add),
                    waits=[(s_ve, C_PRE)], incs=[(s_xr, 16)])
                C_XR = s_xr.count

                pair = [ps[0], ps[1]] if i % 2 == 0 else [ps[2], pr]
                wv_t = [(s_ac, C_OT)]
                if tok_free.get(i % 2, 0) > 0:
                    wv_t.append((s_ve, tok_free[i % 2]))
                for f in range(NF):
                    pt = pair[f // 4]
                    P.emit("tensor", lambda en, pt=pt, f=f, i=i, ot=ot_slab: en.matmul(
                        pt[:, (f % 4) * 128:(f % 4 + 1) * 128],
                        ot[:, f * 512 + i * 128: f * 512 + i * 128 + 128],
                        idbf[:], start=True, stop=False),
                        waits=wv_t if f == 0 else ())
                    P.emit("tensor", lambda en, pt=pt, f=f, sc=sc, i=i: en.matmul(
                        pt[:, (f % 4) * 128:(f % 4 + 1) * 128],
                        ffn16[:, f * S + sc * 512 + i * 128: f * S + sc * 512 + i * 128 + 128],
                        idf16[:], start=False, stop=True),
                        incs=[(s_pe, 1)] if f == NF - 1 else ())
                C_TOK = s_pe.count

                oslot = st_ % 2
                wv_f = [(s_pe, C_TOK), (s_xr, C_XR)]
                if out_used.get(oslot, 0) > 0:
                    wv_f.append((s_st, out_used[oslot]))
                for hlf in range(2):
                    P.emit("vector", lambda en, os_=oslot, xs=xsl_i, hh=hlf, pt=pair[hlf]: en.tensor_add(
                        out_f32[:, os_ * 1024 + hh * 512: os_ * 1024 + (hh + 1) * 512],
                        x_re[:, xs * 1024 + hh * 512: xs * 1024 + (hh + 1) * 512],
                        pt[:, 0:512]),
                        waits=wv_f if hlf == 0 else (), incs=[(s_ve, 1)])
                xre_used[xsl_i] = s_ve.count
                tok_free[i % 2] = s_ve.count
                C_OUT = s_ve.count
                P.emit("sync", lambda en, st_=st_, os_=oslot: en.dma_start(
                    out=y_d[st_ * 128:(st_ + 1) * 128, :],
                    in_=out_f32[:, os_ * 1024:(os_ + 1) * 1024]),
                    waits=[(s_ve, C_OUT)], incs=[(s_st, 16)])
                out_used[oslot] = s_st.count

        P.emit("sync", lambda en: en.nop(), waits=[(s_st, s_st.count)])
        P.finalize()
    return nc


def _host_inputs(L0_attn_x, L0_attn_Wq, L0_attn_Wk, L0_attn_Wv, L0_attn_Wo,
                 L0_ffn_weight, L0_ffn_bias):
    x = np.ascontiguousarray(np.asarray(L0_attn_x, dtype=np.float32))
    common = dict(
        wqT=np.ascontiguousarray(np.asarray(L0_attn_Wq, np.float32).T),
        wkT=np.ascontiguousarray(np.asarray(L0_attn_Wk, np.float32).T),
        wvT=np.ascontiguousarray(np.asarray(L0_attn_Wv, np.float32).T),
        woT=np.ascontiguousarray(np.asarray(L0_attn_Wo, np.float32).T),
        wfT=np.ascontiguousarray(np.asarray(L0_ffn_weight, np.float32).T),
        biasr=np.ascontiguousarray(np.broadcast_to(
            np.asarray(L0_ffn_bias, np.float32)[None, :], (128, H))),
        masks=_tri_masks(),
        idbf=np.eye(128, dtype=ml_dtypes.bfloat16),
        id8=np.eye(128, dtype=ml_dtypes.float8_e4m3),
        idf16=np.eye(128, dtype=np.float16),
        onesbf=np.ones((128, 128), dtype=ml_dtypes.bfloat16),
    )
    return x, common


def kernel(L0_attn_x, L0_attn_Wq, L0_attn_Wk, L0_attn_Wv, L0_attn_Wo,
           L0_ffn_weight, L0_ffn_bias):
    x, common = _host_inputs(L0_attn_x, L0_attn_Wq, L0_attn_Wk, L0_attn_Wv,
                             L0_attn_Wo, L0_ffn_weight, L0_ffn_bias)
    nc = build()
    in_maps = [dict(x=np.ascontiguousarray(x[b]), **common) for b in range(8)]
    res = run_bass_kernel_spmd(nc, in_maps, core_ids=list(range(8)))
    return np.stack([r["y"] for r in res.results], axis=0)


# version 11
# speedup vs baseline: 1.0607x; 1.0028x over previous
import numpy as np
import ml_dtypes
import concourse.bass as bass
import concourse.mybir as mybir
from concourse.bass_utils import run_bass_kernel_spmd
from contextlib import ExitStack

F32 = mybir.dt.float32
BF16 = mybir.dt.bfloat16
FP16 = mybir.dt.float16
FP8 = mybir.dt.float8e4
AF = mybir.ActivationFunctionType
ALU = mybir.AluOpType

S = 2048
H = 1024
NH = 16
NT = 16
NF = 8
SCALE = 64.0 ** -0.5
EPS = 1e-5

ENGS = ("tensor", "vector", "scalar", "sync", "gpsimd")


class Sem:
    def __init__(self, handle):
        self.h = handle
        self.count = 0


class Prog:

    def __init__(self, nc, ctx):
        self.nc = nc
        self.ctx = ctx
        self.streams = {e: [] for e in ENGS}
        self.sems = {}
        self._last_wait = {}

    def sem(self, name):
        if name not in self.sems:
            self.sems[name] = Sem(self.ctx.enter_context(self.nc.semaphore(name)))
        return self.sems[name]

    def emit(self, eng, fn, waits=(), incs=()):
        wl = []
        for s, v in waits:
            key = (eng, s.h.name)
            if v > 0 and self._last_wait.get(key, -1) < v:
                self._last_wait[key] = v
                wl.append((s, v))
        for s, k in incs:
            s.count += k
        self.streams[eng].append((fn, wl, list(incs)))

    def finalize(self):
        block = self.ctx.enter_context(self.nc.Block())
        streams = self.streams

        def runner(name):
            def _run(eng):
                with self.nc.allow_low_precision(
                        reason="intentional bf16 stores; accumulation stays fp32"):
                    for fn, waits, incs in streams[name]:
                        for s, v in waits:
                            eng.wait_ge(s.h, v)
                        inst = fn(eng)
                        for s, k in incs:
                            inst.then_inc(s.h, k)
            return _run

        block.tensor(runner("tensor"))
        block.vector(runner("vector"))
        block.scalar(runner("scalar"))
        block.sync(runner("sync"))
        block.gpsimd(runner("gpsimd"))


def _tri_masks():
    tl = np.arange(128)[:, None]
    sl = np.arange(128)[None, :]
    tri_hi = (tl <= sl).astype(np.float32)
    tri_lo = (tl >= sl).astype(np.float32)
    tri_lo_r0 = np.maximum(tri_lo, np.broadcast_to((tl == 0), (128, 128)).astype(np.float32))
    tri_hi_c0 = np.maximum(tri_hi, np.broadcast_to((sl == 0), (128, 128)).astype(np.float32))
    m = np.zeros((128, 1024), np.float32)
    m[:, 0:128] = tri_hi
    m[:, 128:256] = tri_lo
    m[:, 256:384] = tri_hi_c0
    m[:, 384:512] = tri_lo
    m[:, 512:640] = tri_lo_r0
    m[:, 640:768] = tri_hi
    return m.astype(ml_dtypes.bfloat16)


def build(debug=False, phase_limit=None):
    nc = bass.Bass("TRN2", target_bir_lowering=False)

    x_d = nc.dram_tensor("x", [S, H], F32, kind="ExternalInput").ap()
    wqT_d = nc.dram_tensor("wqT", [H, H], F32, kind="ExternalInput").ap()
    wkT_d = nc.dram_tensor("wkT", [H, H], F32, kind="ExternalInput").ap()
    wvT_d = nc.dram_tensor("wvT", [H, H], F32, kind="ExternalInput").ap()
    woT_d = nc.dram_tensor("woT", [H, H], F32, kind="ExternalInput").ap()
    wfT_d = nc.dram_tensor("wfT", [H, H], F32, kind="ExternalInput").ap()
    biasr_d = nc.dram_tensor("biasr", [128, H], F32, kind="ExternalInput").ap()
    masks_d = nc.dram_tensor("masks", [128, 1024], BF16, kind="ExternalInput").ap()
    idbf_d = nc.dram_tensor("idbf", [128, 128], BF16, kind="ExternalInput").ap()
    id8_d = nc.dram_tensor("id8", [128, 128], FP8, kind="ExternalInput").ap()
    idf16_d = nc.dram_tensor("idf16", [128, 128], FP16, kind="ExternalInput").ap()
    onesbf_d = nc.dram_tensor("onesbf", [128, 128], BF16, kind="ExternalInput").ap()
    y_d = nc.dram_tensor("y", [S, H], F32, kind="ExternalOutput").ap()

    dbg = {}
    if debug:
        for name, shape, dt in [
            ("d_xnT", [128, NF * S], BF16), ("d_xqT", [128, NF * S], FP8),
            ("d_qT", [128, NF * S], BF16), ("d_kT", [128, NF * S], BF16),
            ("d_v", [128, NT * 1040], BF16), ("d_attnT", [128, NF * S], BF16),
            ("d_ffn", [128, NF * S], FP16), ("d_e0", [97, 4 * S], BF16),
            ("d_e0q", [128, 256], BF16), ("d_stats", [128, 6 * NT], F32),
        ]:
            dbg[name] = nc.dram_tensor(name, shape, dt, kind="ExternalOutput").ap()

    with ExitStack() as ctx:
        e = ctx.enter_context
        xnT = e(nc.sbuf_tensor("xnT_attnT", [128, NF * S], BF16))
        attnT = xnT
        qtA = e(nc.sbuf_tensor("qt_ffn", [128, NF * S], BF16))
        QT = qtA
        ffn16 = qtA.bitcast(FP16)
        ktA = e(nc.sbuf_tensor("kt_xre", [128, NF * S], BF16))
        KT = ktA
        x_re = ktA.bitcast(F32)
        vA = e(nc.sbuf_tensor("v_out", [128, NT * 1040], BF16))
        V = vA
        out_f32 = vA.bitcast(F32)
        wbfA = e(nc.sbuf_tensor("wbf_a", [128, NF * H], BF16))
        E0 = wbfA
        wbfB = e(nc.sbuf_tensor("wbf_b", [128, NF * H], BF16))
        wf8 = e(nc.sbuf_tensor("wf8", [128, NF * H], FP8))
        xqT = e(nc.sbuf_tensor("xqT", [128, NF * S], FP8))
        otbf = xqT.bitcast(BF16)

        masks = e(nc.sbuf_tensor("sb_masks", [128, 768], BF16))
        idbf = e(nc.sbuf_tensor("sb_idbf", [128, 128], BF16))
        id8 = e(nc.sbuf_tensor("sb_id8", [128, 128], FP8))
        idf16 = e(nc.sbuf_tensor("sb_idf16", [128, 128], FP16))
        onesbf = e(nc.sbuf_tensor("sb_onesbf", [128, 128], BF16))
        Gc = e(nc.sbuf_tensor("Gc", [128, 8 * 97], BF16))
        Gq = e(nc.sbuf_tensor("Gq", [128, 8 * 16], BF16))
        Vg = e(nc.sbuf_tensor("Vg", [97, 1040], BF16))
        E0qT = e(nc.sbuf_tensor("E0qT", [128, 256], BF16))
        Esb = e(nc.sbuf_tensor("Esb", [128, 6 * 448], BF16))
        scA = e(nc.sbuf_tensor("scA", [128, S], BF16))
        Drow = e(nc.sbuf_tensor("Drow", [65, S], BF16))
        Dsp = e(nc.sbuf_tensor("Dsp", [128, 16], BF16))
        Dspr = e(nc.sbuf_tensor("Dspr", [128, 16], BF16))
        rc = e(nc.sbuf_tensor("rc", [128, 2 * 512], BF16))
        sumc = e(nc.sbuf_tensor("sumc", [128, NT], F32))
        cinvH = e(nc.sbuf_tensor("cinvH", [128, 4], F32))
        cnegH = e(nc.sbuf_tensor("cnegH", [128, 4], F32))
        cepsm = e(nc.sbuf_tensor("cepsm", [128, 4], F32))
        sqc = e(nc.sbuf_tensor("sqc", [128, NT], F32))
        zb = e(nc.sbuf_tensor("zb", [128, 512], BF16))
        mcol = e(nc.sbuf_tensor("mcol", [128, NT], F32))
        vcol = e(nc.sbuf_tensor("vcol", [128, NT], F32))
        tcol = sumc
        rstd = vcol
        nmr = tcol

        pb = [e(nc.psum_tensor(f"pb{i}", [128, 512], F32)) for i in range(4)]
        ps = [e(nc.psum_tensor(f"ps{i}", [128, 512], F32)) for i in range(3)]
        pr = e(nc.psum_tensor("pr", [128, 512], F32))
        psum_rot = pb + ps + [pr]

        P = Prog(nc, ctx)
        s_ldc = P.sem("ldc")
        s_ldx = P.sem("ldx")
        s_ldw = P.sem("ldw")
        s_pe = P.sem("pe")
        s_ve = P.sem("ve")
        s_ac = P.sem("ac")
        s_gp = P.sem("gp")
        s_st = P.sem("st")
        s_vg = P.sem("vg")
        s_xr = P.sem("xr")
        s_dsc = P.sem("dsc")

        biasr = ktA.bitcast(F32)[:, 4096:5120]
        for dst, src in [(masks, masks_d[:, 0:768]), (idbf, idbf_d), (id8, id8_d),
                         (idf16, idf16_d), (onesbf, onesbf_d)]:
            P.emit("sync", lambda en, d=dst, s=src: en.dma_start(out=d[:], in_=s),
                   incs=[(s_ldc, 16)])
        C_CONST = s_ldc.count

        def wload(dst, src):
            return lambda en: en.dma_start(
                out=dst[:].rearrange("p (ic o) -> p ic o", o=H),
                in_=src.rearrange("(ic p) o -> p ic o", p=128))

        P.emit("gpsimd", wload(wbfA, wqT_d), incs=[(s_ldw, 16)])
        CW_Q = s_ldw.count
        P.emit("gpsimd", wload(wbfB, wkT_d), incs=[(s_ldw, 16)])
        CW_K = s_ldw.count
        P.emit("gpsimd", wload(wf8, wfT_d), incs=[(s_ldw, 16)])
        CW_F = s_ldw.count

        x_sb = qtA.bitcast(F32)
        xn_tok = ktA
        xq_tok = ktA.bitcast(FP8)
        xd3 = x_d.rearrange("(c t p) h -> c p t h", p=128, t=4)

        rot_i = 0
        rot_free = [0] * 8
        chunk_done = {}
        P.emit("vector", lambda en: en.memset(cinvH[:], 1.0 / H))
        P.emit("vector", lambda en: en.memset(cnegH[:], -1.0 / H))
        P.emit("vector", lambda en: en.memset(cepsm[:], float(EPS)), incs=[(s_ve, 1)])
        C_MEMS = s_ve.count

        kt_f32 = ktA.bitcast(F32)

        def xslot(c):
            if c == 2:
                return kt_f32[:, 4096:8192]
            sl = 0 if c == 3 else c
            return x_sb[:, sl * 4096:(sl + 1) * 4096]

        xdma_count = {}
        sq_red_count = {}
        for c in range(3):
            P.emit("sync", lambda en, c=c: en.dma_start(
                out=xslot(c).rearrange("p (t h) -> p t h", h=1024), in_=xd3[c]),
                incs=[(s_ldx, 16)])
            xdma_count[c] = s_ldx.count

        for c in range(4):
            if c == 1:
                P.emit("sync", lambda en: en.dma_start(
                    out=xslot(3).rearrange("p (t h) -> p t h", h=1024), in_=xd3[3]),
                    waits=[(s_ve, chunk_done[0])], incs=[(s_ldx, 16)])
                xdma_count[3] = s_ldx.count
            C_X = xdma_count[c]
            xsq_a = ktA.bitcast(F32)[:, 3072:4096]
            xsq_b = vA.bitcast(F32)[:, 0:1024]
            for i in range(4):
                t = 4 * c + i
                xt = xslot(c)[:, i * 1024:(i + 1) * 1024]
                xsq = xsq_a if i % 2 == 0 else xsq_b
                P.emit("vector", lambda en, xt=xt, t=t: en.reduce_sum(
                    sumc[:, t:t + 1], xt, axis=mybir.AxisListType.X),
                    waits=[(s_ldx, C_X)])
                wv_sq = [(s_ldx, C_X)]
                if (c, i) > (0, 1):
                    wv_sq.append((s_ve, sq_red_count[(4 * c + i) - 2]))
                P.emit("scalar", lambda en, xt=xt, xq=xsq: en.activation(
                    xq, xt, AF.Square),
                    waits=wv_sq, incs=[(s_ac, 1)])
                C_SQ = s_ac.count
                P.emit("vector", lambda en, t=t, xq=xsq: en.reduce_sum(
                    sqc[:, t:t + 1], xq, axis=mybir.AxisListType.X),
                    waits=[(s_ac, C_SQ)], incs=[(s_ve, 1)])
                sq_red_count[4 * c + i] = s_ve.count
            C_RED = s_ve.count
            t0, t1 = 4 * c, 4 * c + 4
            sl = slice(t0, t1)
            P.emit("gpsimd", lambda en, sl=sl: en.tensor_mul(mcol[:, sl], sumc[:, sl], cnegH[:, 0:4]),
                   waits=[(s_ve, C_RED), (s_ve, C_MEMS)])
            P.emit("gpsimd", lambda en, sl=sl: en.tensor_mul(tcol[:, sl], mcol[:, sl], mcol[:, sl]))
            P.emit("gpsimd", lambda en, sl=sl: en.tensor_mul(vcol[:, sl], sqc[:, sl], cinvH[:, 0:4]))
            P.emit("gpsimd", lambda en, sl=sl: en.tensor_add(vcol[:, sl], vcol[:, sl], cepsm[:, 0:4]))
            P.emit("gpsimd", lambda en, sl=sl: en.tensor_sub(vcol[:, sl], vcol[:, sl], tcol[:, sl]),
                   incs=[(s_gp, 1)])
            C_V = s_gp.count
            P.emit("vector", lambda en, sl=sl: en.reciprocal(tcol[:, sl], vcol[:, sl]),
                   waits=[(s_gp, C_V)], incs=[(s_ve, 1)])
            C_RV = s_ve.count
            P.emit("scalar", lambda en, sl=sl: en.activation(rstd[:, sl], tcol[:, sl], AF.Sqrt),
                   waits=[(s_ve, C_RV)], incs=[(s_ac, 1)])
            C_RS = s_ac.count
            P.emit("gpsimd", lambda en, sl=sl: en.tensor_mul(nmr[:, sl], mcol[:, sl], rstd[:, sl]),
                   waits=[(s_ac, C_RS)], incs=[(s_gp, 1)])
            C_NMR = s_gp.count

            for i in range(4):
                t = 4 * c + i
                xt = xslot(c)[:, i * 1024:(i + 1) * 1024]
                P.emit("scalar", lambda en, xt=xt, i=i, t=t: en.activation(
                    xn_tok[:, i * 1024:(i + 1) * 1024], xt, AF.Identity,
                    scale=rstd[:, t:t + 1], bias=nmr[:, t:t + 1]),
                    waits=[(s_gp, C_NMR)])
                P.emit("scalar", lambda en, xt=xt, i=i, t=t: en.activation(
                    xq_tok[:, 8192 + i * 1024: 8192 + (i + 1) * 1024], xt, AF.Identity,
                    scale=rstd[:, t:t + 1], bias=nmr[:, t:t + 1]),
                    incs=[(s_ac, 1)])
            C_LN = s_ac.count

            for kind in range(2):
                for f in range(NF):
                    bi = rot_i % 8
                    bank = psum_rot[bi]
                    for i in range(4):
                        wv = []
                        if i == 0:
                            wv = [(s_ac, C_LN), (s_ldc, C_CONST)]
                            if rot_free[bi] > 0:
                                wv.append((s_ve, rot_free[bi]))
                        if kind == 0:
                            lhsT = xn_tok[:, i * 1024 + f * 128:i * 1024 + (f + 1) * 128]
                            rhs = idbf[:]
                        else:
                            lhsT = xq_tok[:, 8192 + i * 1024 + f * 128: 8192 + i * 1024 + (f + 1) * 128]
                            rhs = id8[:]
                        P.emit("tensor", lambda en, b=bank, i=i, l=lhsT, r=rhs: en.matmul(
                            b[:, i * 128:(i + 1) * 128], l, r, start=True, stop=True),
                            waits=wv, incs=[(s_pe, 1)] if i == 3 else ())
                    C_T = s_pe.count
                    dst = (xnT if kind == 0 else xqT)[:, f * S + c * 512: f * S + (c + 1) * 512]
                    P.emit("vector", lambda en, b=bank, d=dst: en.tensor_copy(d, b[:]),
                           waits=[(s_pe, C_T)], incs=[(s_ve, 1)])
                    rot_free[bi] = s_ve.count
                    rot_i += 1
            chunk_done[c] = s_ve.count
            C_P0ACT = s_ac.count
        C_P0 = s_ve.count
        C_DBG_P0 = 0
        if debug:
            def dump_stats(en):
                return en.dma_start(out=dbg["d_stats"].rearrange("p (k t) -> p k t", t=NT),
                                    in_=bass.AP(sumc, 0, [[NT, 128], [0, 1], [1, NT]]))
            for ii, tens in enumerate([sumc, sqc, mcol, vcol, rstd, nmr]):
                P.emit("gpsimd", lambda en, ii=ii, tn=tens: en.dma_start(
                    out=dbg["d_stats"][:, ii * NT:(ii + 1) * NT], in_=tn[:]),
                    waits=[(s_ve, C_P0)], incs=[(s_vg, 16)])
            P.emit("gpsimd", lambda en: en.dma_start(out=dbg["d_xnT"], in_=xnT[:]),
                   waits=[(s_ve, C_P0)], incs=[(s_vg, 16)])
            P.emit("gpsimd", lambda en: en.dma_start(out=dbg["d_xqT"], in_=xqT[:]),
                   incs=[(s_vg, 16)])
            C_DBG_P0 = s_vg.count

        if phase_limit == "p0":
            P.emit("sync", lambda en: en.nop(), waits=[(s_vg, s_vg.count)])
            P.finalize()
            return nc

        bank_free = {0: 0, 1: 0}

        def gemm_T(wsb, out_sb, wait_w):
            for o in range(NF):
                banks = [psum_rot[(o % 2) * 4 + j] for j in range(4)]
                for ic in range(NF):
                    lhsT = wsb[:, ic * H + o * 128: ic * H + (o + 1) * 128]
                    for sc in range(4):
                        wv = []
                        if ic == 0 and sc == 0:
                            wv = [(s_ldw, wait_w), (s_ve, C_P0)]
                            if bank_free[o % 2] > 0:
                                wv.append((s_ve, bank_free[o % 2]))
                        P.emit("tensor", lambda en, b=banks[sc], l=lhsT, sc=sc, ic=ic: en.matmul(
                            b[:], l, xnT[:, ic * S + sc * 512: ic * S + (sc + 1) * 512],
                            start=(ic == 0), stop=(ic == NF - 1)),
                            waits=wv,
                            incs=[(s_pe, 1)] if (ic == NF - 1 and sc == 3) else ())
                C_G = s_pe.count
                for sc in range(4):
                    dst = out_sb[:, o * S + sc * 512: o * S + (sc + 1) * 512]
                    P.emit("vector", lambda en, b=banks[sc], d=dst: en.tensor_copy(d, b[:]),
                           waits=[(s_pe, C_G)] if sc == 0 else (), incs=[(s_ve, 1)])
                bank_free[o % 2] = s_ve.count
            return s_ve.count

        C_QT = gemm_T(wbfA, QT, CW_Q)
        P.emit("gpsimd", wload(wbfA, wvT_d),
               waits=[(s_pe, s_pe.count)], incs=[(s_ldw, 16)])
        CW_V = s_ldw.count
        C_KT = gemm_T(wbfB, KT, CW_K)
        P.emit("gpsimd", wload(wbfB, woT_d),
               waits=[(s_pe, s_pe.count)], incs=[(s_ldw, 16)])
        CW_O = s_ldw.count

        P.emit("vector", lambda en: en.memset(V[:], 1.0), waits=[(s_ac, C_P0ACT)])
        P.emit("vector", lambda en: en.memset(zb[:], 0.0), incs=[(s_ve, 1)])
        C_VONE = s_ve.count
        vbank_free = {0: 0, 1: 0}
        C_V0 = 0
        for t in range(NT):
            banks = [psum_rot[(t % 2) * 2 + j] for j in range(2)]
            for ic in range(NF):
                lhsT = xnT[:, ic * S + t * 128: ic * S + (t + 1) * 128]
                for oc in range(2):
                    wv = []
                    if ic == 0 and oc == 0:
                        wv = [(s_ldw, CW_V), (s_ve, C_VONE), (s_ve, C_KT)]
                        if vbank_free[t % 2] > 0:
                            wv.append((s_ve, vbank_free[t % 2]))
                    P.emit("tensor", lambda en, b=banks[oc], l=lhsT, ic=ic, oc=oc: en.matmul(
                        b[:], l, wbfA[:, ic * H + oc * 512: ic * H + (oc + 1) * 512],
                        start=(ic == 0), stop=(ic == NF - 1)),
                        waits=wv,
                        incs=[(s_pe, 1)] if (ic == NF - 1 and oc == 1) else ())
            C_VG = s_pe.count
            for oc in range(2):
                dstap = V[:, t * 1040 + oc * 520: t * 1040 + (oc + 1) * 520].rearrange(
                    "p (h d) -> p h d", d=65)[:, :, 0:64]
                srcap = banks[oc][:].rearrange("p (h d) -> p h d", d=64)
                zap = zb[:].rearrange("p (h d) -> p h d", d=64)
                P.emit("vector", lambda en, sp=srcap, dp=dstap, zp=zap: en.tensor_tensor(dp, sp, zp, ALU.add),
                       waits=[(s_pe, C_VG)] if oc == 0 else (), incs=[(s_ve, 1)])
            vbank_free[t % 2] = s_ve.count
            if t == 0:
                C_V0 = s_ve.count
        C_VALL = s_ve.count
        C_DBG_P1 = 0
        if debug:
            P.emit("gpsimd", lambda en: en.dma_start(out=dbg["d_qT"], in_=QT[:]),
                   waits=[(s_ve, C_VALL)], incs=[(s_vg, 16)])
            P.emit("gpsimd", lambda en: en.dma_start(out=dbg["d_kT"], in_=KT[:]),
                   incs=[(s_vg, 16)])
            P.emit("gpsimd", lambda en: en.dma_start(out=dbg["d_v"], in_=V[:]),
                   incs=[(s_vg, 16)])
            C_DBG_P1 = s_vg.count

        for j in range(4):
            P.emit("gpsimd", lambda en, j=j: en.dma_start(out=Vg[32 * j:32 * j + 1, :], in_=V[0:1, 0:1040]),
                   waits=[(s_ve, C_V0)] if j == 0 else (), incs=[(s_vg, 16)])
        C_VGC = s_vg.count

        P.emit("vector", lambda en: en.memset(Gc[:], 0.0), waits=[(s_ve, C_KT)])
        P.emit("vector", lambda en: en.memset(Gq[:], 0.0))
        for cch in range(8):
            colA = 32 * ((2 * cch) % 4)
            colB = 32 * ((2 * cch + 1) % 4)
            P.emit("vector", lambda en, c=cch, col=colA: en.tensor_copy(
                Gc[0:64, c * 97 + col: c * 97 + col + 1], KT[0:64, c * S: c * S + 1]))
            P.emit("vector", lambda en, c=cch, col=colB: en.tensor_copy(
                Gc[64:128, c * 97 + col: c * 97 + col + 1], KT[64:128, c * S: c * S + 1]))
            P.emit("vector", lambda en, c=cch: en.tensor_copy(
                Gq[0:64, c * 16 + 2 * c: c * 16 + 2 * c + 1], QT[0:64, c * S: c * S + 1]),
                waits=[(s_ve, C_QT)] if cch == 0 else ())
            P.emit("vector", lambda en, c=cch: en.tensor_copy(
                Gq[64:128, c * 16 + 2 * c + 1: c * 16 + 2 * c + 2], QT[64:128, c * S: c * S + 1]),
                incs=[(s_ve, 1)] if cch == 7 else ())
        C_GBUILD = s_ve.count

        for tt in range(NT):
            for cch in range(NF):
                wv = []
                if tt == 0 and cch == 0:
                    wv = [(s_ve, C_GBUILD), (s_ve, C_VALL)]
                P.emit("tensor", lambda en, tt=tt, c=cch: en.matmul(
                    ps[0][:, tt * 16:(tt + 1) * 16],
                    KT[:, c * S + tt * 128: c * S + (tt + 1) * 128],
                    Gq[:, c * 16:(c + 1) * 16],
                    start=(c == 0), stop=(c == NF - 1)),
                    waits=wv,
                    incs=[(s_pe, 1)] if (tt == NT - 1 and cch == NF - 1) else ())
        C_S0Q = s_pe.count
        P.emit("scalar", lambda en: en.activation(E0qT[:], ps[0][:, 0:256], AF.Exp, scale=float(SCALE)),
               waits=[(s_pe, C_S0Q)], incs=[(s_ac, 1)])
        C_E0Q = s_ac.count

        e0_wait = {0: 0, 1: 0}
        for wv_i in range(4):
            for sc in range(4):
                k = wv_i * 4 + sc
                bank = pr if k % 2 == 0 else ps[1]
                wvl = [(s_ve, C_GBUILD)]
                if e0_wait[k % 2] > 0:
                    wvl.append((s_ac, e0_wait[k % 2]))
                for ci, cch in enumerate((2 * wv_i, 2 * wv_i + 1)):
                    P.emit("tensor", lambda en, c=cch, sc=sc, b=bank, ci=ci: en.matmul(
                        b[0:97, :], Gc[:, c * 97:(c + 1) * 97],
                        QT[:, c * S + sc * 512: c * S + (sc + 1) * 512],
                        start=(ci == 0), stop=(ci == 1)),
                        waits=wvl if ci == 0 else (), incs=[(s_pe, 1)] if ci == 1 else ())
                C_E0M = s_pe.count
                P.emit("scalar", lambda en, w=wv_i, sc=sc, b=bank: en.activation(
                    E0[0:97, w * S + sc * 512: w * S + (sc + 1) * 512], b[0:97, :],
                    AF.Exp, scale=float(SCALE)),
                    waits=[(s_pe, C_E0M)], incs=[(s_ac, 1)])
                e0_wait[k % 2] = s_ac.count
        C_E0 = s_ac.count
        if debug:
            P.emit("gpsimd", lambda en: en.dma_start(out=dbg["d_e0"], in_=E0[0:97, 0:4 * S]),
                   waits=[(s_ac, C_E0)], incs=[(s_vg, 16)])
            P.emit("gpsimd", lambda en: en.dma_start(out=dbg["d_e0q"], in_=E0qT[:]),
                   waits=[(s_ac, C_E0Q)], incs=[(s_vg, 16)])

        ps_free = [C_E0Q, e0_wait[1], 0]
        ebuf_free = [0, 0, 0]
        pb_free = [0, 0, 0, 0]
        pr_free = [0]
        pending_norm = []
        last_rep_pe = {-1: 0}
        chain_done = {}
        recip_done = {-1: 0}
        chain_out2 = {-1: 0}

        def emit_norm_job(h2, b2):
            hr2 = (h2 % 2) * 64
            hf2 = h2 // 2
            wv = [(s_dsc, chain_done[h2]), (s_ldc, C_CONST), (s_ac, C_E0)]
            if pr_free[0] != 0:
                wv.append((s_ac, pr_free[0][1]))
            P.emit("tensor", lambda en, b=b2, po=hr2: en.matmul(
                pr[po:po + 64, :], onesbf[64:65, 0:64], Drow[64:65, b * 512:(b + 1) * 512],
                start=True, stop=True, tile_position=(64, po)),
                waits=wv, incs=[(s_pe, 1)])
            C_REP = s_pe.count
            last_rep_pe[h2] = C_REP
            P.emit("scalar", lambda en, b=b2, po=hr2: en.copy(
                rc[po:po + 64, (b % 2) * 512:(b % 2 + 1) * 512], pr[po:po + 64, :]),
                waits=[(s_pe, C_REP)], incs=[(s_ac, 1)])
            pr_free[0] = ("ac", s_ac.count)
            dst = attnT[hr2:hr2 + 64, hf2 * S + b2 * 512: hf2 * S + (b2 + 1) * 512]
            wv2 = [(s_ac, s_ac.count)]
            if debug and C_DBG_P0 > 0:
                wv2.append((s_vg, C_DBG_P0))
            P.emit("vector", lambda en, d=dst, b=b2, po=hr2: en.tensor_tensor(
                d, scA[po:po + 64, b * 512:(b + 1) * 512],
                rc[po:po + 64, (b % 2) * 512:(b % 2 + 1) * 512], ALU.mult),
                waits=wv2, incs=[(s_ve, 1)])

        NSTEP = NH * NT
        scores_head_done = {}
        DEPTH = 5
        sinfo = {}
        touched_h = {}
        ebuf_free = [0, 0, 0, 0, 0, 0]

        def emit_front(st):
            h, tc = divmod(st, NT)
            hr = (h % 2) * 64
            hf = h // 2
            g = st % 3
            slot = st % 6
            if tc == 0:
                q0, q1, ocol, ncol = 0, 2, 128, 256
            elif tc == NT - 1:
                q0, q1, ocol, ncol = tc - 1, tc + 1, 0, 256
            else:
                q0, q1, ocol, ncol = tc - 1, tc + 2, 0, 384
            wv = [(s_ve, C_VALL), (s_vg, C_VGC), (s_ac, C_E0)]
            if ps_free[g] > 0:
                wv.append((s_ac, ps_free[g]))
            lhsT = KT[hr:hr + 64, hf * S + tc * 128: hf * S + (tc + 1) * 128]
            rhs = QT[hr:hr + 64, hf * S + q0 * 128: hf * S + q1 * 128]
            P.emit("tensor", lambda en, b=ps[g], l=lhsT, r=rhs, oc=ocol, nn=ncol, hr=hr: en.matmul(
                b[:, oc:oc + nn], l, r, start=True, stop=True, tile_position=(hr, 0)),
                waits=wv, incs=[(s_pe, 1)])
            C_SC = s_pe.count
            if tc == NT - 1:
                scores_head_done[h] = C_SC

            eoff, ewid = (128, 256) if tc == 0 else ((0, 256) if tc == NT - 1 else (0, 384))
            wv = [(s_pe, C_SC)]
            if ebuf_free[slot] > 0:
                wv.append((s_pe, ebuf_free[slot]))
            P.emit("scalar", lambda en, b=ps[g], slot=slot, eo=eoff, ew=ewid: en.activation(
                Esb[:, slot * 448 + eo: slot * 448 + eo + ew], b[:, eo:eo + ew],
                AF.Exp, scale=float(SCALE)),
                waits=wv, incs=[(s_ac, 1)])
            ps_free[g] = s_ac.count
            C_EXP = s_ac.count

            eng = "vector" if st % 2 == 0 else "gpsimd"
            sem = s_ve if eng == "vector" else s_gp
            if tc == 0:
                eap = Esb[:, slot * 448 + 256: slot * 448 + 384]
                map_ = masks[:, 512:640]
            elif tc == 1:
                eap = Esb[:, slot * 448: slot * 448 + 384].rearrange("p (a b) -> p a b", b=128)[:, 0:3:2, :]
                map_ = masks[:, 256:512].rearrange("p (a b) -> p a b", b=128)
            elif tc == NT - 1:
                eap = Esb[:, slot * 448: slot * 448 + 128]
                map_ = masks[:, 640:768]
            else:
                eap = Esb[:, slot * 448: slot * 448 + 384].rearrange("p (a b) -> p a b", b=128)[:, 0:3:2, :]
                map_ = masks[:, 0:256].rearrange("p (a b) -> p a b", b=128)
            P.emit(eng, lambda en, eap=eap, map_=map_: en.tensor_tensor(eap, eap, map_, ALU.mult),
                   waits=[(s_ac, C_EXP), (s_ldc, C_CONST)], incs=[(sem, 1)])
            sinfo[st] = (h, tc, slot, sem, sem.count)

        def emit_back(st):
            h, tc, slot, mask_sem, mask_cnt = sinfo.pop(st)
            hr = (h % 2) * 64
            hf = h // 2
            er0 = 32 * (h % 4)
            wv0 = h // 4
            if tc == 0:
                touched_h[h] = [False, False, False, False]
            touched = touched_h[h]
            if tc == 0:
                blocks = [(128, 0), (256, 1)]
            elif tc == NT - 1:
                blocks = [(0, tc - 1), (128, tc)]
            else:
                blocks = [(0, tc - 1), (128, tc), (256, tc + 1)]
            runs = []
            for ecol, Tq in blocks:
                b = Tq // 4
                col = (Tq % 4) * 128
                fresh = (Tq == tc + 1)
                if (runs and not fresh and runs[-1][1] == b
                        and runs[-1][2] + runs[-1][3] == col and not runs[-1][4]):
                    runs[-1] = (runs[-1][0], b, runs[-1][2], runs[-1][3] + 128, False)
                elif (runs and tc == 0 and runs[-1][1] == b
                      and runs[-1][2] + runs[-1][3] == col):
                    runs[-1] = (runs[-1][0], b, runs[-1][2], runs[-1][3] + 128, True)
                else:
                    runs.append((ecol, b, col, 128, fresh))
            lhsT_v = V[:, tc * 1040 + h * 65: tc * 1040 + (h + 1) * 65]
            first = True
            for ecol, b, col, width, _fresh in runs:
                wv = []
                if first:
                    wv.append((mask_sem, mask_cnt))
                    first = False
                st_flag = not touched[b]
                if st_flag and pb_free[b] > 0:
                    wv.append((s_ve, pb_free[b]))
                touched[b] = True
                P.emit("tensor", lambda en, bb=pb[b], l=lhsT_v, slot=slot, ec=ecol, col=col, wd=width, sf=st_flag: en.matmul(
                    bb[0:65, col:col + wd], l,
                    Esb[:, slot * 448 + ec: slot * 448 + ec + wd],
                    start=sf, stop=False),
                    waits=wv, incs=[(s_pe, 1)])
            if tc >= 2:
                P.emit("tensor", lambda en, l=lhsT_v, tc=tc, h=h: en.matmul(
                    pb[0][0:65, 0:1], l, E0qT[:, tc * 16 + h: tc * 16 + h + 1],
                    start=False, stop=(tc == 15)),
                    waits=[(s_ac, C_E0Q)], incs=[(s_pe, 1)])
            e0b = []
            if tc == 4:
                e0b = [(0, 256, 256, False)]
            elif tc == 8:
                e0b = [(1, 0, 512, True)]
            elif tc == 12:
                e0b = [(2, 0, 512, True)]
            elif tc == 15:
                e0b = [(3, 0, 512, True)]
            for b, col, width, stp in e0b:
                P.emit("tensor", lambda en, b=b, col=col, wd=width, stp=stp, er=er0, w=wv0, h=h: en.matmul(
                    pb[b][0:65, col:col + wd],
                    Vg[er:er + 1, h * 65:(h + 1) * 65],
                    E0[er:er + 1, w * S + b * 512 + col: w * S + b * 512 + col + wd],
                    start=False, stop=stp, tile_position=(er, 0)),
                    incs=[(s_pe, 1)])
            ebuf_free[slot] = s_pe.count

            if tc in (7, 9, 11, 13) and pending_norm:
                h2, b2 = pending_norm.pop(0)
                emit_norm_job(h2, b2)


            jobs = []
            if tc == 8:
                jobs = [1]
            elif tc == 12:
                jobs = [2]
            elif tc == 15:
                jobs = [0, 3]
            for b in jobs:
                C_AVB = s_pe.count
                wv_e = [(s_pe, C_AVB)]
                if h > 0 and tc == 8:
                    wv_e.append((s_dsc, chain_done[h - 1]))
                P.emit("vector", lambda en, b=b, hr=hr: en.tensor_copy(
                    scA[hr:hr + 64, b * 512:(b + 1) * 512], pb[b][0:64, :]),
                    waits=wv_e)
                P.emit("vector", lambda en, b=b: en.tensor_copy(
                    Drow[64:65, b * 512:(b + 1) * 512], pb[b][64:65, :]),
                    incs=[(s_ve, 1)])
                pb_free[b] = s_ve.count
            if tc == 15:
                C_EV = s_ve.count
                drow_ap = bass.AP(Drow, 64 * S, [[S, 1], [1, S]])
                P.emit("sync", lambda en, ap=drow_ap: en.dma_start(out=Dsp[:], in_=ap),
                       waits=[(s_ve, C_EV), (s_ve, recip_done[h - 1])], incs=[(s_dsc, 16)])
                c1 = s_dsc.count
                P.emit("vector", lambda en: en.reciprocal(Dspr[:], Dsp[:]),
                       waits=[(s_dsc, c1), (s_dsc, chain_out2[h - 1])], incs=[(s_ve, 1)])
                recip_done[h] = s_ve.count
                P.emit("sync", lambda en, ap=drow_ap: en.dma_start(out=ap, in_=Dspr[:]),
                       waits=[(s_ve, recip_done[h]), (s_pe, last_rep_pe[h - 1])],
                       incs=[(s_dsc, 16)])
                chain_out2[h] = s_dsc.count
                chain_done[h] = s_dsc.count
                for b in (1, 2, 0, 3):
                    pending_norm.append((h, b))


        for st_i in range(NSTEP + DEPTH):
            if st_i < NSTEP:
                emit_front(st_i)
            if st_i >= DEPTH:
                emit_back(st_i - DEPTH)
        while pending_norm:
            h2, b2 = pending_norm.pop(0)
            emit_norm_job(h2, b2)
        C_ATTN = s_ve.count
        C_DBG_P2 = 0
        if debug:
            P.emit("gpsimd", lambda en: en.dma_start(out=dbg["d_attnT"], in_=attnT[:]),
                   waits=[(s_ve, C_ATTN)], incs=[(s_vg, 16)])
            C_DBG_P2 = s_vg.count

        P.emit("sync", lambda en: en.dma_start(out=biasr, in_=biasr_d),
               waits=[(s_ve, C_ATTN)], incs=[(s_ldc, 16)])
        C_BIAS = s_ldc.count
        fbank_free = {0: 0, 1: 0}
        for f in range(NF):
            banks = [psum_rot[(f % 2) * 4 + j] for j in range(4)]
            for cc in range(NF):
                lhsT = wf8[:, cc * H + f * 128: cc * H + (f + 1) * 128]
                for sc in range(4):
                    wv = []
                    if cc == 0 and sc == 0:
                        wv = [(s_ldw, CW_F), (s_ve, C_ATTN)]
                        if debug and C_DBG_P2 > 0:
                            wv.append((s_vg, C_DBG_P2))
                        if fbank_free[f % 2] > 0:
                            wv.append((s_ac, fbank_free[f % 2]))
                    P.emit("tensor", lambda en, b=banks[sc], l=lhsT, cc=cc, sc=sc: en.matmul(
                        b[:], l, xqT[:, cc * S + sc * 512: cc * S + (sc + 1) * 512],
                        start=(cc == 0), stop=(cc == NF - 1)),
                        waits=wv,
                        incs=[(s_pe, 1)] if (cc == NF - 1 and sc == 3) else ())
            C_FG = s_pe.count
            for sc in range(4):
                dst = ffn16[:, f * S + sc * 512: f * S + (sc + 1) * 512]
                P.emit("scalar", lambda en, b=banks[sc], d=dst: en.copy(d, b[:]),
                       waits=[(s_pe, C_FG)] if sc == 0 else (), incs=[(s_ac, 1)])
            fbank_free[f % 2] = s_ac.count
        C_FFN = s_ac.count
        if debug:
            P.emit("gpsimd", lambda en: en.dma_start(out=dbg["d_ffn"], in_=ffn16[:, 0:NF * S]),
                   waits=[(s_ac, C_FFN)], incs=[(s_vg, 16)])

        if debug:
            P.emit("gpsimd", lambda en: en.dma_start(out=dbg["d_ffn"], in_=ffn16[:, 0:NF * S]),
                   waits=[(s_ac, C_FFN)], incs=[(s_vg, 16)])

        wo_bank_free = {}
        xre_used = {}
        tok_free = {}
        out_used = {}
        for sc in range(4):
            ot_slab = otbf[:, (sc % 2) * 4096:(sc % 2) * 4096 + 4096]
            for wave in range(2):
                for oo in range(4):
                    o = wave * 4 + oo
                    for hc in range(NF):
                        lhsT = wbfB[:, hc * H + o * 128: hc * H + (o + 1) * 128]
                        wv = []
                        if hc == 0:
                            wv = [(s_ldw, CW_O), (s_ve, C_ATTN), (s_ac, C_FFN)]
                            if wo_bank_free.get(oo, 0) > 0:
                                wv.append((s_ac, wo_bank_free[oo]))
                        P.emit("tensor", lambda en, b=pb[oo], l=lhsT, hc=hc, sc=sc: en.matmul(
                            b[:], l, attnT[:, hc * S + sc * 512: hc * S + (sc + 1) * 512],
                            start=(hc == 0), stop=(hc == NF - 1)),
                            waits=wv,
                            incs=[(s_pe, 1)] if hc == NF - 1 else ())
                    C_WO = s_pe.count
                    dsl = ot_slab[:, o * 512:(o + 1) * 512]
                    P.emit("scalar", lambda en, b=pb[oo], d=dsl: en.copy(d, b[:]),
                           waits=[(s_pe, C_WO)], incs=[(s_ac, 1)])
                    wo_bank_free[oo] = s_ac.count
            C_OT = s_ac.count

            for i in range(4):
                st_ = sc * 4 + i
                xsl_i = st_ % 4
                wv = [(s_ldc, C_BIAS), (s_ac, C_FFN)]
                if xre_used.get(xsl_i, 0) > 0:
                    wv.append((s_ve, xre_used[xsl_i]))
                P.emit("vector", lambda en, xs=xsl_i: en.tensor_copy(
                    x_re[:, xs * 1024:(xs + 1) * 1024], biasr),
                    waits=wv, incs=[(s_ve, 1)])
                C_PRE = s_ve.count
                P.emit("gpsimd", lambda en, st_=st_, xs=xsl_i: en.dma_start(
                    out=x_re[:, xs * 1024:(xs + 1) * 1024],
                    in_=x_d[st_ * 128:(st_ + 1) * 128, :], accum_op=ALU.add),
                    waits=[(s_ve, C_PRE)], incs=[(s_xr, 16)])
                C_XR = s_xr.count

                pair = [ps[0], ps[1]] if i % 2 == 0 else [ps[2], pr]
                wv_t = [(s_ac, C_OT)]
                if tok_free.get(i % 2, 0) > 0:
                    wv_t.append((s_ve, tok_free[i % 2]))
                for f in range(NF):
                    pt = pair[f // 4]
                    P.emit("tensor", lambda en, pt=pt, f=f, i=i, ot=ot_slab: en.matmul(
                        pt[:, (f % 4) * 128:(f % 4 + 1) * 128],
                        ot[:, f * 512 + i * 128: f * 512 + i * 128 + 128],
                        idbf[:], start=True, stop=False),
                        waits=wv_t if f == 0 else ())
                    P.emit("tensor", lambda en, pt=pt, f=f, sc=sc, i=i: en.matmul(
                        pt[:, (f % 4) * 128:(f % 4 + 1) * 128],
                        ffn16[:, f * S + sc * 512 + i * 128: f * S + sc * 512 + i * 128 + 128],
                        idf16[:], start=False, stop=True),
                        incs=[(s_pe, 1)] if f == NF - 1 else ())
                C_TOK = s_pe.count

                oslot = st_ % 2
                wv_f = [(s_pe, C_TOK), (s_xr, C_XR)]
                if out_used.get(oslot, 0) > 0:
                    wv_f.append((s_st, out_used[oslot]))
                for hlf in range(2):
                    P.emit("vector", lambda en, os_=oslot, xs=xsl_i, hh=hlf, pt=pair[hlf]: en.tensor_add(
                        out_f32[:, os_ * 1024 + hh * 512: os_ * 1024 + (hh + 1) * 512],
                        x_re[:, xs * 1024 + hh * 512: xs * 1024 + (hh + 1) * 512],
                        pt[:, 0:512]),
                        waits=wv_f if hlf == 0 else (), incs=[(s_ve, 1)])
                xre_used[xsl_i] = s_ve.count
                tok_free[i % 2] = s_ve.count
                C_OUT = s_ve.count
                P.emit("sync", lambda en, st_=st_, os_=oslot: en.dma_start(
                    out=y_d[st_ * 128:(st_ + 1) * 128, :],
                    in_=out_f32[:, os_ * 1024:(os_ + 1) * 1024]),
                    waits=[(s_ve, C_OUT)], incs=[(s_st, 16)])
                out_used[oslot] = s_st.count

        P.emit("sync", lambda en: en.nop(), waits=[(s_st, s_st.count)])
        P.finalize()
    return nc


def _host_inputs(L0_attn_x, L0_attn_Wq, L0_attn_Wk, L0_attn_Wv, L0_attn_Wo,
                 L0_ffn_weight, L0_ffn_bias):
    x = np.ascontiguousarray(np.asarray(L0_attn_x, dtype=np.float32))
    common = dict(
        wqT=np.ascontiguousarray(np.asarray(L0_attn_Wq, np.float32).T),
        wkT=np.ascontiguousarray(np.asarray(L0_attn_Wk, np.float32).T),
        wvT=np.ascontiguousarray(np.asarray(L0_attn_Wv, np.float32).T),
        woT=np.ascontiguousarray(np.asarray(L0_attn_Wo, np.float32).T),
        wfT=np.ascontiguousarray(np.asarray(L0_ffn_weight, np.float32).T),
        biasr=np.ascontiguousarray(np.broadcast_to(
            np.asarray(L0_ffn_bias, np.float32)[None, :], (128, H))),
        masks=_tri_masks(),
        idbf=np.eye(128, dtype=ml_dtypes.bfloat16),
        id8=np.eye(128, dtype=ml_dtypes.float8_e4m3),
        idf16=np.eye(128, dtype=np.float16),
        onesbf=np.ones((128, 128), dtype=ml_dtypes.bfloat16),
    )
    return x, common


def kernel(L0_attn_x, L0_attn_Wq, L0_attn_Wk, L0_attn_Wv, L0_attn_Wo,
           L0_ffn_weight, L0_ffn_bias):
    x, common = _host_inputs(L0_attn_x, L0_attn_Wq, L0_attn_Wk, L0_attn_Wv,
                             L0_attn_Wo, L0_ffn_weight, L0_ffn_bias)
    nc = build()
    in_maps = [dict(x=np.ascontiguousarray(x[b]), **common) for b in range(8)]
    res = run_bass_kernel_spmd(nc, in_maps, core_ids=list(range(8)))
    return np.stack([r["y"] for r in res.results], axis=0)
